# version 01
# baseline (speedup 1.0000x reference)
import bisect
import numpy as np
from contextlib import ExitStack

import concourse.bass as bass
import concourse.mybir as mybir
from concourse.bass_utils import run_bass_kernel_spmd

F32 = mybir.dt.float32
BF16 = mybir.dt.bfloat16
ALU = mybir.AluOpType
AF = mybir.ActivationFunctionType
AX = mybir.AxisListType

D = 2048
T = 8192
H = 16
FF = 5632
E = 8
NDC = 16
NFC = 44
NST = 16
QSCALE = float(1.0 / np.sqrt(128.0))
EXPOFF = -12.0
MASKW = 2432
NEG = -1e30

ENGS = ("tensor", "vector", "scalar", "gpsimd", "sync")
SMALL_IN_AV = False
C_MIN = False


def _tr4(t, ps, off, slices, ident):
    n = len(slices)
    for k, sl in enumerate(slices):
        mm = t.matmul(ps[:, off + 128 * k:off + 128 * (k + 1)], sl, ident,
                      is_transpose=True, start=(k == 0), stop=(k == n - 1))
    return mm


def _ap(t, off, dims):
    tt = getattr(t, "tensor", None)
    return bass.AP(tt if tt is not None and not callable(tt) else t, off, dims)


class Op:
    __slots__ = ("eng", "fn", "dma", "idx", "deps", "dma_ord")

    def __init__(self, eng, fn, dma):
        self.eng, self.fn, self.dma = eng, fn, dma
        self.deps = {}
        self.dma_ord = 0


class Sched:

    def __init__(self, nc, sems):
        self.nc = nc
        self.sems = sems
        self.ops = {e: [] for e in ENGS}
        self.res = {}

    def add(self, eng, fn, reads=(), writes=(), dma=False):
        op = Op(eng, fn, dma)
        op.idx = len(self.ops[eng])
        if dma:
            op.dma_ord = sum(1 for o in self.ops[eng] if o.dma)
        self.ops[eng].append(op)

        def dep(w):
            if w is None:
                return
            peng, pidx, pdma, pord = w
            if peng == eng and not pdma:
                return
            key = (peng, pdma)
            val = pord if pdma else pidx
            if op.deps.get(key, -1) < val:
                op.deps[key] = val

        for r in reads:
            ent = self.res.get(r)
            if ent is None:
                continue
            dep(ent[0])
            ent[1].append((eng, op.idx, dma, op.dma_ord))
        me = (eng, op.idx, dma, op.dma_ord)
        for w in writes:
            ent = self.res.setdefault(w, [None, []])
            dep(ent[0])
            for rd in ent[1]:
                dep(rd)
            ent[0] = me
            ent[1] = []
        return op

    def emit(self, tag):
        nc = self.nc
        needed = {}
        for c in ENGS:
            for op in self.ops[c]:
                for (p, pdma), val in op.deps.items():
                    if not pdma:
                        needed.setdefault(p, set()).add(val)
        needed = {k: sorted(v) for k, v in needed.items()}
        psem = {p: self.sems(f"{tag}_p{p[:2]}") for p in needed}
        marked = {p: set(v) for p, v in needed.items()}
        dsem = {e: self.sems(f"{tag}_d{e[:2]}")
                for e in ENGS if any(o.dma for o in self.ops[e])}
        ndma = {e: sum(1 for o in self.ops[e] if o.dma) for e in dsem}
        waited = {}

        def emit_eng(eng_obj, e):
            for op in self.ops[e]:
                for (p, pdma), val in sorted(op.deps.items(),
                                             key=lambda kv: str(kv[0])):
                    if pdma:
                        sem, wv = dsem[p], 16 * (val + 1)
                    else:
                        sem = psem[p]
                        wv = bisect.bisect_right(needed[p], val)
                    key = (e, sem.num)
                    if waited.get(key, -1) >= wv:
                        continue
                    waited[key] = wv
                    eng_obj.wait_ge(sem, wv)
                inst = op.fn(eng_obj)
                if op.dma:
                    inst.then_inc(dsem[e], 16)
                elif e in marked and op.idx in marked[e]:
                    eng_obj.drain().then_inc(psem[e], 1)
            if e == "sync":
                for p, sem in dsem.items():
                    eng_obj.wait_ge(sem, 16 * ndma[p])

        with nc.Block() as blk:
            for e in ENGS:
                if self.ops[e] or e == "sync":
                    getattr(blk, e)(lambda eng_obj, e=e: emit_eng(eng_obj, e))
        nc.all_engine_barrier()


class Build:
    def __init__(self, phases=99, debug=False):
        self.phases = phases
        self.debug = debug
        nc = bass.Bass("TRN2", target_bir_lowering=False)
        self.nc = nc
        self.stack = ExitStack()
        self._semn = 0
        dram = lambda n, s, d, k="ExternalInput": nc.dram_tensor(n, s, d, kind=k).ap()
        sk = "ExternalOutput" if debug else "Internal"
        self.x_kv = dram("x_kv", [T, D], F32)
        self.xq = dram("xq", [2048, D], F32)
        self.x_own = dram("x_own", [2048, D], F32)
        self.strip = dram("strip", [128, MASKW], F32)
        for nm, sh in [("wq", [D, D]), ("wk", [D, D]), ("wv", [D, D]),
                       ("wo", [D, D]), ("bq", [1, D]), ("bk", [1, D]),
                       ("bv", [1, D]), ("bo", [1, D]), ("gwt", [D, E]),
                       ("w1t", [D, FF]), ("w3t", [D, FF]), ("w2t", [FF, D]),
                       ("a1h", [128, NDC * E * 16]), ("a3h", [128, NDC * E * 16]),
                       ("a2h", [128, E * NFC * 16]), ("b1h", [16, E * FF]),
                       ("b3h", [16, E * FF]), ("b2h", [16, E * D])]:
            setattr(self, nm, dram(nm, sh, F32))
        self.y = dram("y", [2048, D], F32, "ExternalOutput")
        self.xT_s = dram("xT_s", [128, NDC * T], BF16, sk)
        self.kT_s = dram("kT_s", [128, H * T], BF16, sk)
        self.v_s = dram("v_s", [T, D], BF16, sk)
        self.qT_s = dram("qT_s", [128, H * 2048], BF16, sk)
        self.x1_s = dram("x1_s", [2048, D], F32, sk)
        self.x1T_s = dram("x1T_s", [128, NDC * 2048], BF16, sk)
        self.w1c = dram("w1c", [D, FF], BF16, sk)
        self.w3c = dram("w3c", [D, FF], BF16, sk)
        self.w2c = dram("w2c", [FF, D], BF16, sk)
        self.a1c = dram("a1c", [128, NDC * E * 16], BF16, sk)
        self.a3c = dram("a3c", [128, NDC * E * 16], BF16, sk)
        self.a2c = dram("a2c", [128, E * NFC * 16], BF16, sk)
        self.b1c = dram("b1c", [16, E * FF], BF16, sk)
        self.b3c = dram("b3c", [16, E * FF], BF16, sk)
        self.b2c = dram("b2c", [16, E * D], BF16, sk)
        self.b1_s = dram("b1_s", [FF, 512], BF16, sk)
        self.b3_s = dram("b3_s", [FF, 512], BF16, sk)
        if debug:
            self.attnT_d = dram("attnT_d", [128, H * 2048], BF16, "ExternalOutput")
            self.hres_d = dram("hres_d", [2048, D], F32, "ExternalOutput")
            self.st_d = dram("st_d", [2048, 8], F32, "ExternalOutput")
            self.acc_d = dram("acc_d", [512, NDC * 512], F32, "ExternalOutput")
            self.a_d = dram("a_d", [128, NFC * 512], BF16, "ExternalOutput")
            self.u_d = dram("u_d", [16, 1536], BF16, "ExternalOutput")
            self.wf_d = dram("wf_d", [128, 16 * E], F32, "ExternalOutput")
            self.wT_d = dram("wT_d", [8, 2048], F32, "ExternalOutput")

    def sems(self, name):
        self._semn += 1
        return self.stack.enter_context(
            self.nc.semaphore(f"{name}{self._semn}"))

    def sb(self, st, name, shape, dtype):
        return st.enter_context(self.nc.sbuf_tensor(name, shape, dtype))

    def psum(self, st, name, shape, dtype=F32):
        return st.enter_context(self.nc.psum_tensor(name, shape, dtype))

    def build(self):
        nc = self.nc
        with self.stack:
            ps = self.stack
            self.ident_b = self.sb(ps, "ident_b", [128, 128], BF16)
            self.ident_f = self.sb(ps, "ident_f", [128, 128], F32)
            self.ones_b = self.sb(ps, "ones_b", [1, 512], BF16)
            self.onec_b = self.sb(ps, "onec_b", [128, 1], BF16)
            self.wf_sb = self.sb(ps, "wf_sb", [128, 16 * E], F32)
            self.cbias = self.sb(ps, "cbias", [128, 2], F32)
            self.gapb = self.sb(ps, "gapb", [128, 4], F32)
            self.junkrow = (self.sb(ps, "junkrow", [16, 1536], BF16)
                            if self.debug else None)
            self.wT_sb = self.sb(ps, "wT_sb", [8, 2048], F32)
            s0 = self.stack.enter_context(nc.semaphore("s00"))
            self.first_sem = s0.num
            nc.gpsimd.dma_reset(range(self.first_sem, 256))
            nc.gpsimd.sem_clear(range(self.first_sem, 256))
            nc.all_engine_barrier()
            self.d_consts()
            if self.phases >= 1:
                with ExitStack() as st:
                    self.d_ak(st)
            if self.phases >= 2:
                with ExitStack() as st:
                    self.d_av(st)
            if self.phases >= 3:
                with ExitStack() as st:
                    self.d_aq(st)
            if self.phases >= 4:
                with ExitStack() as bc:
                    attnT = self.sb(bc, "attnT", [128, H * 2048], BF16)
                    with ExitStack() as st:
                        self.d_b(st, attnT)
                    if self.debug:
                        self.d_dump(self.attnT_d, attnT, "dmpB")
                    if self.phases >= 5:
                        with ExitStack() as st:
                            self.d_c(st, attnT)
            if self.phases >= 5 and self.debug:
                self.d_dump(self.wf_d, self.wf_sb, "dmpW")
                self.d_dump(self.wT_d, self.wT_sb, "dmpT")
            if self.phases >= 6:
                with ExitStack() as st:
                    self.d_f(st)
            nc.gpsimd.sem_clear(range(self.first_sem, 256))
            nc.all_engine_barrier()
        return nc

    def d_dump(self, dst, src, tag):
        nc = self.nc
        sd = self.sems(tag)
        with nc.Block() as blk:
            @blk.sync
            def _(sy):
                sy.dma_start(dst[:, :], src[:, :]).then_inc(sd, 16)
                sy.wait_ge(sd, 16)
        nc.all_engine_barrier()

    def d_consts(self):
        S = Sched(self.nc, self.sems)

        def consts(g):
            g.memset(self.ones_b[:, :], 1.0)
            g.memset(self.onec_b[:, :], 1.0)
            for ident in (self.ident_b, self.ident_f):
                g.memset(ident[:, :], 0.0)
                g.affine_select(ident[:, :], ident[:, :], [[-1, 128]],
                                ALU.not_equal, 1.0, base=0,
                                channel_multiplier=1)
            g.memset(self.cbias[:, 0:1], EXPOFF)
            return g.memset(self.cbias[:, 1:2], 1e-5)
        S.add("gpsimd", consts, writes=["consts"])
        S.emit("cn")

    def _castload(self, S, dst_sb, src, n, tag, stg, cols=D):
        nb = 2 if stg[0] is not stg[1] else 1
        for i in range(n):
            par = i % nb
            S.add("sync",
                  lambda sy, i=i, par=par: sy.dma_start(
                      stg[par][:, 0:cols], src[128 * i:128 * (i + 1), :]),
                  writes=[f"{tag}stg{par}"], dma=True)
            S.add("vector",
                  lambda v, i=i, par=par: v.tensor_copy(
                      dst_sb[:, cols * i:cols * (i + 1)], stg[par][:, 0:cols]),
                  reads=[f"{tag}stg{par}"], writes=[f"{tag}w{i}"])

    def _bias(self, S, dst, src, tag, stg):
        S.add("sync",
              lambda sy: sy.dma_start(_ap(stg[0], 0, [[D, 1], [1, D]]),
                                      src[:, :]),
              writes=[f"{tag}stg0"], dma=True)
        S.add("vector", lambda v: v.tensor_copy(dst[:, :], stg[0][0:1, 0:D]),
              reads=[f"{tag}stg0"], writes=[f"{tag}b"])

    def d_ak(self, st):
        S = Sched(self.nc, self.sems)
        w_sb = self.sb(st, "Ak_w", [128, NDC * D], BF16)
        bb = self.sb(st, "Ak_bb", [1, D], BF16)
        stg = [self.sb(st, f"Ak_stg{i}", [128, D], F32) for i in range(2)]
        xst = [self.sb(st, f"Ak_x{i}", [128, 4 * D], BF16) for i in range(2)]
        xT = [self.sb(st, f"Ak_xT{i}", [128, NDC * 512], BF16) for i in range(2)]
        kev = self.sb(st, "Ak_kev", [128, H * 512], BF16)
        pst = self.psum(st, "Ak_pst", [128, 2048], BF16)
        psk = self.psum(st, "Ak_psk", [128, 1024], F32)
        self._castload(S, w_sb, self.wk, NDC, "Ak", stg)
        self._bias(S, bb, self.bk, "Ak", stg)
        wreads = [f"Akw{i}" for i in range(NDC)] + ["Akb"]
        for i in range(NST):
            for sub in range(4):
                n = 4 * i + sub
                S.add("sync",
                      lambda sy, i=i, sub=sub, n=n: sy.dma_start(
                          stg[n % 2][:, :],
                          self.x_kv[512 * i + 128 * sub:
                                    512 * i + 128 * (sub + 1), :]),
                      writes=[f"Akstg{n % 2}"], dma=True)
                S.add("vector",
                      lambda v, i=i, sub=sub, n=n: v.tensor_copy(
                          xst[i % 2][:, D * sub:D * (sub + 1)],
                          stg[n % 2][:, :]),
                      reads=[f"Akstg{n % 2}"], writes=[f"Akx{i % 2}s{sub}"])
            for dc in range(NDC):
                ph = f"Akpst{dc % 2}"

                def tr(t, i=i, dc=dc):
                    return _tr4(t, pst, 1024 * (dc % 2),
                                [xst[i % 2][:, D * sub + 128 * dc:
                                            D * sub + 128 * (dc + 1)]
                                 for sub in range(4)], self.ident_b[:, :])
                S.add("tensor", tr,
                      reads=[f"Akx{i % 2}s{s}" for s in range(4)] + ["consts"],
                      writes=[ph])
                S.add("vector",
                      lambda v, i=i, dc=dc: v.tensor_copy(
                          xT[i % 2][:, 512 * dc:512 * (dc + 1)],
                          pst[:, 1024 * (dc % 2):1024 * (dc % 2) + 512]),
                      reads=[ph], writes=[f"AkxT{i % 2}c{dc}"])
            xreads = [f"AkxT{i % 2}c{dc}" for dc in range(NDC)]
            S.add("sync",
                  lambda sy, i=i: sy.dma_start(
                      _ap(self.xT_s, 512 * i,
                          [[NDC * T, 128], [T, NDC], [1, 512]]),
                      xT[i % 2][:, :]),
                  reads=xreads, writes=[f"xT_s{i}"], dma=True)
            for fc in range(H):
                pb = f"Akpsk{fc % 2}"

                def kp(t, i=i, fc=fc):
                    bank = psk[:, 512 * (fc % 2):512 * (fc % 2 + 1)]
                    for dc in range(NDC):
                        t.matmul(bank,
                                 w_sb[:, D * dc + 128 * fc:
                                      D * dc + 128 * (fc + 1)],
                                 xT[i % 2][:, 512 * dc:512 * (dc + 1)],
                                 start=(dc == 0), stop=False)
                    return t.matmul(bank, bb[:, 128 * fc:128 * (fc + 1)],
                                    self.ones_b[:, :], start=False, stop=True)
                S.add("tensor", kp, reads=xreads + wreads + ["consts"],
                      writes=[pb])
                S.add("scalar",
                      lambda s, fc=fc: s.activation(
                          kev[:, 512 * fc:512 * (fc + 1)],
                          psk[:, 512 * (fc % 2):512 * (fc % 2 + 1)], AF.Copy),
                      reads=[pb], writes=[f"Akkev{fc}"])
                S.add("sync",
                      lambda sy, i=i, fc=fc: sy.dma_start(
                          self.kT_s[:, T * fc + 512 * i:T * fc + 512 * (i + 1)],
                          kev[:, 512 * fc:512 * (fc + 1)]),
                      reads=[f"Akkev{fc}"], writes=["kT_s"], dma=True)
        S.emit("ak")

    def d_av(self, st):
        S = Sched(self.nc, self.sems)
        w_sb = self.sb(st, "Av_w", [128, NDC * D], BF16)
        bb = self.sb(st, "Av_bb", [1, D], BF16)
        stg = [self.sb(st, f"Av_stg{i}", [128, D], F32) for i in range(2)]
        cst = self.sb(st, "Av_cst", [128, D], BF16)
        xT = [self.sb(st, f"Av_xT{i}", [128, NDC * 512], BF16) for i in range(2)]
        vev = [self.sb(st, f"Av_vev{i}", [128, 4 * D], BF16) for i in range(2)]
        psv = self.psum(st, "Av_psv", [128, 1024], F32)
        self._castload(S, w_sb, self.wv, NDC, "Av", stg)
        self._bias(S, bb, self.bv, "Av", stg)
        wreads = [f"Avw{i}" for i in range(NDC)] + ["Avb"]
        jobs = []
        for srcn, dstn, rows, tot in [("a1h", "a1c", 128, NDC * E * 16),
                                      ("a3h", "a3c", 128, NDC * E * 16),
                                      ("a2h", "a2c", 128, E * NFC * 16),
                                      ("b1h", "b1c", 16, E * FF),
                                      ("b3h", "b3c", 16, E * FF),
                                      ("b2h", "b2c", 16, E * D)]:
            src, dst = getattr(self, srcn), getattr(self, dstn)
            off = 0
            while off < tot:
                w = min(2048, tot - off)
                jobs.append((src, dst, rows, off, w, tot))
                off += w

        def smalljob(k):
            src, dst, rows, off, w, tot = jobs[k]
            par = k % 2
            S.add("sync",
                  lambda sy: sy.dma_start(
                      _ap(stg[par], 0, [[D, rows], [1, w]]),
                      _ap(src, off, [[tot, rows], [1, w]])),
                  writes=[f"Avstg{par}"], dma=True)
            S.add("vector",
                  lambda v: v.tensor_copy(
                      _ap(cst, 0, [[D, rows], [1, w]]),
                      _ap(stg[par], 0, [[D, rows], [1, w]])),
                  reads=[f"Avstg{par}"], writes=["Avcst"])
            S.add("gpsimd",
                  lambda g: g.dma_start(
                      _ap(dst, off, [[tot, rows], [1, w]]),
                      _ap(cst, 0, [[D, rows], [1, w]])),
                  reads=["Avcst"], writes=["smallc"], dma=True)

        job_k = [0]

        def pump():
            if job_k[0] < len(jobs):
                smalljob(job_k[0])
                job_k[0] += 1

        for i in range(NST):
            S.add("sync",
                  lambda sy, i=i: sy.dma_start(
                      xT[i % 2][:, :],
                      _ap(self.xT_s, 512 * i,
                          [[NDC * T, 128], [T, NDC], [1, 512]])),
                  reads=[f"xT_s{i}"], writes=[f"AvxT{i % 2}"], dma=True)
            for sub in range(4):
                for fc in range(4):
                    g4 = 4 * sub + fc
                    pb = f"Avps{g4 % 2}"

                    def vp(t, i=i, sub=sub, fc=fc, g4=g4):
                        bank = psv[:, 512 * (g4 % 2):512 * (g4 % 2 + 1)]
                        for dc in range(NDC):
                            t.matmul(bank,
                                     xT[i % 2][:, 512 * dc + 128 * sub:
                                               512 * dc + 128 * (sub + 1)],
                                     w_sb[:, D * dc + 512 * fc:
                                          D * dc + 512 * (fc + 1)],
                                     start=(dc == 0), stop=False)
                        return t.matmul(bank, self.ones_b[:, 0:128],
                                        bb[:, 512 * fc:512 * (fc + 1)],
                                        start=False, stop=True)
                    S.add("tensor", vp,
                          reads=[f"AvxT{i % 2}"] + wreads + ["consts"],
                          writes=[pb])
                    S.add("scalar",
                          lambda s, i=i, sub=sub, fc=fc, g4=g4: s.activation(
                              vev[i % 2][:, D * sub + 512 * fc:
                                         D * sub + 512 * (fc + 1)],
                              psv[:, 512 * (g4 % 2):512 * (g4 % 2 + 1)],
                              AF.Copy),
                          reads=[pb], writes=[f"Avev{i % 2}s{sub}f{fc}"])
                S.add("sync",
                      lambda sy, i=i, sub=sub: sy.dma_start(
                          self.v_s[512 * i + 128 * sub:
                                   512 * i + 128 * (sub + 1), :],
                          vev[i % 2][:, D * sub:D * (sub + 1)]),
                      reads=[f"Avev{i % 2}s{sub}f{fc}" for fc in range(4)],
                      writes=["v_s"], dma=True)
        if SMALL_IN_AV:
            while job_k[0] < len(jobs):
                pump()
        S.emit("av")
        if not SMALL_IN_AV:
            S2 = Sched(self.nc, self.sems)
            S_sav, self_jobs = S, jobs
            def smalljob2(S, k):
                src, dst, rows, off, w, tot = jobs[k]
                par = k % 2
                S.add("sync",
                      lambda sy: sy.dma_start(
                          _ap(stg[par], 0, [[D, rows], [1, w]]),
                          _ap(src, off, [[tot, rows], [1, w]])),
                      writes=[f"Avstg{par}"], dma=True)
                S.add("vector",
                      lambda v: v.tensor_copy(
                          _ap(cst, 0, [[D, rows], [1, w]]),
                          _ap(stg[par], 0, [[D, rows], [1, w]])),
                      reads=[f"Avstg{par}"], writes=["Avcst"])
                S.add("gpsimd",
                      lambda g: g.dma_start(
                          _ap(dst, off, [[tot, rows], [1, w]]),
                          _ap(cst, 0, [[D, rows], [1, w]])),
                      reads=["Avcst"], writes=["smallc"], dma=True)
            for k in range(len(jobs)):
                smalljob2(S2, k)
            S2.emit("sm")

    def d_aq(self, st):
        S = Sched(self.nc, self.sems)
        w_sb = self.sb(st, "Aq_w", [128, NDC * D], BF16)
        bb = self.sb(st, "Aq_bb", [1, D], BF16)
        stg = [self.sb(st, f"Aq_stg{i}", [128, D], F32) for i in range(2)]
        xqb = [self.sb(st, f"Aq_x{i}", [128, D], BF16) for i in range(2)]
        xqT = [self.sb(st, f"Aq_xT{i}", [128, D], BF16) for i in range(2)]
        qpe = [self.sb(st, f"Aq_qpe{i}", [128, D], BF16) for i in range(2)]
        qT = [self.sb(st, f"Aq_qT{i}", [128, 2048], BF16) for i in range(2)]
        pst = self.psum(st, "Aq_pst", [128, 2048], BF16)
        psq = self.psum(st, "Aq_psq", [128, 1024], F32)
        self._castload(S, w_sb, self.wq, NDC, "Aq", stg)
        self._bias(S, bb, self.bq, "Aq", stg)
        wreads = [f"Aqw{i}" for i in range(NDC)] + ["Aqb"]
        for h in range(H):
            n = NDC + h
            S.add("sync",
                  lambda sy, h=h, n=n: sy.dma_start(
                      stg[n % 2][:, :], self.xq[128 * h:128 * (h + 1), :]),
                  writes=[f"Aqstg{n % 2}"], dma=True)
            S.add("vector",
                  lambda v, h=h, n=n: v.tensor_copy(xqb[h % 2][:, :],
                                                    stg[n % 2][:, :]),
                  reads=[f"Aqstg{n % 2}"], writes=[f"Aqx{h % 2}"])
            for b4 in range(4):
                ph = f"Aqpst{b4 % 2}"

                def tr(t, h=h, b4=b4):
                    return _tr4(t, pst, 1024 * (b4 % 2),
                                [xqb[h % 2][:, 128 * (4 * b4 + k):
                                            128 * (4 * b4 + k + 1)]
                                 for k in range(4)], self.ident_b[:, :])
                S.add("tensor", tr, reads=[f"Aqx{h % 2}", "consts"],
                      writes=[ph])
                S.add("vector",
                      lambda v, h=h, b4=b4: v.tensor_copy(
                          xqT[h % 2][:, 512 * b4:512 * (b4 + 1)],
                          pst[:, 1024 * (b4 % 2):1024 * (b4 % 2) + 512]),
                      reads=[ph], writes=[f"AqxT{h % 2}b{b4}"])
            xreads = [f"AqxT{h % 2}b{b}" for b in range(4)]
            for fc in range(4):
                pb = f"Aqpsq{fc % 2}"

                def qp(t, h=h, fc=fc):
                    bank = psq[:, 512 * (fc % 2):512 * (fc % 2 + 1)]
                    for dc in range(NDC):
                        t.matmul(bank, xqT[h % 2][:, 128 * dc:128 * (dc + 1)],
                                 w_sb[:, D * dc + 512 * fc:
                                      D * dc + 512 * (fc + 1)],
                                 start=(dc == 0), stop=False)
                    return t.matmul(bank, self.ones_b[:, 0:128],
                                    bb[:, 512 * fc:512 * (fc + 1)],
                                    start=False, stop=True)
                S.add("tensor", qp, reads=xreads + wreads + ["consts"],
                      writes=[pb])
                S.add("scalar",
                      lambda s, h=h, fc=fc: s.activation(
                          qpe[h % 2][:, 512 * fc:512 * (fc + 1)],
                          psq[:, 512 * (fc % 2):512 * (fc % 2 + 1)],
                          AF.Copy, scale=QSCALE),
                      reads=[pb], writes=[f"Aqpe{h % 2}f{fc}"])
            for sub in range(H):
                ph = f"Aqp2_{sub % 2}"
                S.add("tensor",
                      lambda t, h=h, sub=sub: t.transpose(
                          pst[:, 1024 * (sub % 2):1024 * (sub % 2) + 128],
                          qpe[h % 2][:, 128 * sub:128 * (sub + 1)],
                          self.ident_b[:, :]),
                      reads=[f"Aqpe{h % 2}f{sub // 4}", "consts"], writes=[ph])
                S.add("vector",
                      lambda v, h=h, sub=sub: v.tensor_copy(
                          _ap(qT[h % 2], sub, [[2048, 128], [512, 4], [16, 32]]),
                          _ap(pst, 1024 * (sub % 2),
                              [[2048, 128], [32, 4], [1, 32]])),
                      reads=[ph], writes=[f"AqqT{h % 2}s{sub}"])
            S.add("sync",
                  lambda sy, h=h: sy.dma_start(
                      self.qT_s[:, 2048 * h:2048 * (h + 1)], qT[h % 2][:, :]),
                  reads=[f"AqqT{h % 2}s{s}" for s in range(H)],
                  writes=["qT_s"], dma=True)
        S.emit("aq")

    def d_b(self, st, attnT):
        S = Sched(self.nc, self.sems)
        strip_sb = self.sb(st, "strip_sb", [128, MASKW], F32)
        kT = [self.sb(st, f"b_kT{i}", [128, T], BF16) for i in range(2)]
        vb = [self.sb(st, f"b_v{i}", [128, T], BF16) for i in range(2)]
        qT = [self.sb(st, f"b_qT{i}", [128, 2048], BF16) for i in range(2)]
        P = [self.sb(st, f"b_P{i}", [128, 1024], BF16) for i in range(2)]
        lnl = self.sb(st, "b_lnl", [1, 512], F32)
        rrow = self.sb(st, "b_rrow", [1, 512], F32)
        rhi = self.sb(st, "b_rhi", [1, 512], BF16)
        rlo_f = self.sb(st, "b_rlo_f", [1, 512], F32)
        rlo = self.sb(st, "b_rlo", [1, 512], BF16)
        rb = self.sb(st, "b_rb", [128, 512], F32)
        wp_stg = self.sb(st, "wp_stg", [128, FF], F32)
        wp_bf = self.sb(st, "wp_bf", [128, FF], BF16)
        psc = self.psum(st, "b_psc", [128, 2048], F32)
        pav = self.psum(st, "b_pav", [128, 512], F32)
        pl = self.psum(st, "b_pl", [128, 512], F32)
        prb = self.psum(st, "b_prb", [128, 512], F32)

        S.add("sync", lambda sy: sy.dma_start(strip_sb[:, :], self.strip[:, :]),
              writes=["strip"], dma=True)
        jobs = ([(self.w1t, self.w1c, i, FF) for i in range(NDC)]
                + [(self.w3t, self.w3c, i, FF) for i in range(NDC)]
                + [(self.w2t, self.w2c, i, D) for i in range(NFC)])

        def wpjob(k):
            src, dst, i, cols = jobs[k]
            S.add("sync",
                  lambda sy: sy.dma_start(wp_stg[:, 0:cols],
                                          src[128 * i:128 * (i + 1), :]),
                  writes=["wps"], dma=True)
            S.add("vector",
                  lambda v: v.tensor_copy(wp_bf[:, 0:cols], wp_stg[:, 0:cols]),
                  reads=["wps"], writes=["wpb"])
            S.add("sync",
                  lambda sy: sy.dma_start(dst[128 * i:128 * (i + 1), :],
                                          wp_bf[:, 0:cols]),
                  reads=["wpb"], writes=["wXc"], dma=True)

        job_k = [0]

        def pump():
            if job_k[0] < len(jobs):
                wpjob(job_k[0])
                job_k[0] += 1

        for h in range(H):
            S.add("sync",
                  lambda sy, h=h: sy.dma_start(
                      kT[h % 2][:, :], self.kT_s[:, T * h:T * (h + 1)]),
                  reads=["kT_s"], writes=[f"bkT{h % 2}"], dma=True)
            S.add("sync",
                  lambda sy, h=h: sy.dma_start(
                      vb[h % 2][:, :],
                      _ap(self.v_s, 128 * h,
                          [[D, 128], [128 * D, T // 128], [1, 128]])),
                  reads=["v_s"], writes=[f"bv{h % 2}"], dma=True)
            S.add("sync",
                  lambda sy, h=h: sy.dma_start(
                      qT[h % 2][:, :], self.qT_s[:, 2048 * h:2048 * (h + 1)]),
                  reads=["qT_s"], writes=[f"bq{h % 2}"], dma=True)
            for stp in range(4):
                gs = 4 * h + stp
                ncs = 8 * (stp + 1)

                def sc_op(t, h=h, stp=stp, c=0):
                    for i in range(2):
                        k = 2 * c + i
                        mm = t.matmul(
                            psc[:, 1024 * (c % 2) + 512 * i:
                                1024 * (c % 2) + 512 * (i + 1)],
                            kT[h % 2][:, 128 * k:128 * (k + 1)],
                            qT[h % 2][:, 512 * stp:512 * (stp + 1)],
                            start=True, stop=True)
                    return mm

                def mk_op(v, stp=stp, c=0):
                    for i in range(2):
                        vk = 2 * c + i - 16 * stp
                        off = 1920 - 128 * vk
                        sl = psc[:, 1024 * (c % 2) + 512 * i:
                                 1024 * (c % 2) + 512 * (i + 1)]
                        mm = v.tensor_tensor(sl, sl, strip_sb[:, off:off + 512],
                                             ALU.add)
                    return mm

                def ex_op(s, c=0):
                    return s.activation(
                        P[c % 2][:, :],
                        psc[:, 1024 * (c % 2):1024 * (c % 2) + 1024],
                        AF.Exp, bias=self.cbias[:, 0:1])

                def av_op(t, h=h, c=0, ncs=ncs):
                    first, last = (c == 0), (c == ncs - 1)
                    for i in range(2):
                        k = 2 * c + i
                        t.matmul(pav[:, :], vb[h % 2][:, 128 * k:128 * (k + 1)],
                                 P[c % 2][:, 512 * i:512 * (i + 1)],
                                 start=(first and i == 0),
                                 stop=(last and i == 1))
                    for i in range(2):
                        mm = t.matmul(pl[0:1, 0:512], self.onec_b[:, :],
                                      P[c % 2][:, 512 * i:512 * (i + 1)],
                                      start=(first and i == 0),
                                      stop=(last and i == 1))
                    return mm

                for c in range(ncs):
                    pb = f"bpsc{c % 2}"
                    S.add("tensor",
                          lambda t, h=h, stp=stp, c=c: sc_op(t, h, stp, c),
                          reads=[f"bkT{h % 2}", f"bq{h % 2}"], writes=[pb])
                    if c >= 8 * stp:
                        S.add("vector",
                              lambda v, stp=stp, c=c: mk_op(v, stp, c),
                              reads=[pb, "strip"], writes=[pb])
                    S.add("scalar", lambda s, c=c: ex_op(s, c),
                          reads=[pb], writes=[f"bP{c % 2}", pb])
                    S.add("tensor",
                          lambda t, h=h, c=c, ncs=ncs: av_op(t, h, c, ncs),
                          reads=[f"bP{c % 2}", f"bv{h % 2}", "consts"],
                          writes=["bpav", "bpl", f"bP{c % 2}"])

                def ep_act(s):
                    s.activation(lnl[:, :], pl[0:1, 0:512], AF.Ln)
                    s.activation(rrow[:, :], lnl[:, :], AF.Exp, scale=-1.0)
                    return s.activation(rhi[:, :], rrow[:, :], AF.Copy)
                S.add("scalar", ep_act, reads=["bpl"], writes=["brr", "bpl"])

                def ep_lo(v):
                    v.tensor_sub(rlo_f[:, :], rrow[:, :], rhi[:, :])
                    v.tensor_copy(self.gapb[0:1, 0:4], self.gapb[0:1, 0:4])
                    v.tensor_copy(self.gapb[:, 2:3], self.gapb[:, 0:1])
                    return v.tensor_copy(rlo[:, :], rlo_f[:, :])
                S.add("vector", ep_lo, reads=["brr"], writes=["brlo"])

                def ep_bc(t):
                    t.matmul(prb[:, :], self.ones_b[:, 0:128], rhi[:, :],
                             start=True, stop=False)
                    return t.matmul(prb[:, :], self.ones_b[:, 0:128], rlo[:, :],
                                    start=False, stop=True)
                S.add("tensor", ep_bc,
                      reads=["brr", "brlo", "consts"], writes=["bprb"])

                def ep_ev(v, h=h, stp=stp):
                    v.tensor_copy(rb[:, :], prb[:, :])
                    return v.tensor_tensor(
                        attnT[:, 2048 * h + 512 * stp:
                              2048 * h + 512 * (stp + 1)],
                        pav[:, :], rb[:, :], ALU.mult)
                S.add("vector", ep_ev,
                      reads=["bprb", "bpav"],
                      writes=["attnT", "bpav", "bprb"])
                pump()
                if gs % 2 == 0:
                    pump()
        while job_k[0] < len(jobs):
            pump()
        S.emit("b")

    def d_c(self, st, attnT):
        S = Sched(self.nc, self.sems)
        wo_sb = self.sb(st, "C_wo", [128, NDC * D], BF16)
        bob = self.sb(st, "C_bob", [1, D], BF16)
        stg0 = self.sb(st, "C_stg0", [128, D], F32)
        stg = [stg0, stg0]
        gwb = self.sb(st, "C_gwb", [128, NDC * E], BF16)
        xo = [self.sb(st, f"C_xo{i}", [128, D], F32) for i in range(2)]
        x1lo = self.sb(st, "C_x1lo", [128, D], BF16)
        x1Tl = self.sb(st, "C_x1Tl", [128, D], BF16)
        gwlo = self.sb(st, "C_gwlo", [128, NDC * E], BF16)
        x1f = [self.sb(st, f"C_x1f{i}", [128, D], F32) for i in range(2)]
        x1b = self.sb(st, "C_x1b", [128, D], BF16)
        x1T = [self.sb(st, f"C_x1T{i}", [128, D], BF16) for i in range(2)]
        s1 = self.sb(st, "C_s1", [128, 1], F32)
        mean = self.sb(st, "C_mean", [128, 1], F32)
        m2 = self.sb(st, "C_m2", [128, 1], F32)
        ssq = self.sb(st, "C_ssq", [128, 1], F32)
        var = self.sb(st, "C_var", [128, 1], F32)
        sd = self.sb(st, "C_sd", [128, 1], F32)
        rstd = self.sb(st, "C_rstd", [128, 1], F32)
        gt = self.sb(st, "C_gt", [128, E], F32)
        gd = self.sb(st, "C_gd", [128, E], F32)
        geq1 = self.sb(st, "C_geq1", [128, E], F32)
        geq2 = self.sb(st, "C_geq2", [128, E], F32)
        gm1 = self.sb(st, "C_gm1", [128, 1], F32)
        gm2 = self.sb(st, "C_gm2", [128, 1], F32)
        gdm = self.sb(st, "C_gdm", [128, 1], F32)
        gg1 = self.sb(st, "C_gg1", [128, 1], F32)
        gg2 = self.sb(st, "C_gg2", [128, 1], F32)
        psw = self.psum(st, "C_psw", [128, 2048], F32)
        psg = self.psum(st, "C_psg", [128, 512], F32)
        pst = self.psum(st, "C_pst", [128, 2048], BF16)
        pwt = self.psum(st, "C_pwt", [128, 128], F32)

        self._castload(S, wo_sb, self.wo, NDC, "C", stg)
        self._bias(S, bob, self.bo, "C", stg)
        S.add("sync",
              lambda sy: sy.dma_start(
                  _ap(stg[0], 0, [[D, 128], [E, NDC], [1, E]]),
                  _ap(self.gwt, 0, [[E, 128], [128 * E, NDC], [1, E]])),
              writes=["Cstg0"], dma=True)
        def gwcast(v):
            v.tensor_copy(gwb[:, :], stg[0][:, 0:NDC * E])
            v.tensor_copy(self.gapb[:, 2:3], self.gapb[:, 0:1])
            v.tensor_copy(self.gapb[:, 3:4], self.gapb[:, 0:1])
            return v.tensor_sub(gwlo[:, :], stg[0][:, 0:NDC * E], gwb[:, :])
        S.add("vector", gwcast, reads=["Cstg0"], writes=["Cgw"])
        wreads = [f"Cw{i}" for i in range(NDC)] + ["Cb"]
        for ti in range(16):
            S.add("sync",
                  lambda sy, ti=ti: sy.dma_start(
                      xo[ti % 2][:, :], self.x_own[128 * ti:128 * (ti + 1), :]),
                  writes=[f"Cxo{ti % 2}", "Chres"], dma=True)

            def wo_op(t, ti=ti):
                for oc in range(4):
                    bank = psw[:, 512 * oc:512 * (oc + 1)]
                    for hd in range(NDC):
                        t.matmul(bank,
                                 attnT[:, 2048 * hd + 128 * ti:
                                       2048 * hd + 128 * (ti + 1)],
                                 wo_sb[:, D * hd + 512 * oc:
                                       D * hd + 512 * (oc + 1)],
                                 start=(hd == 0), stop=False)
                    mm = t.matmul(bank, self.ones_b[:, 0:128],
                                  bob[:, 512 * oc:512 * (oc + 1)],
                                  start=False, stop=True)
                return mm
            S.add("tensor", wo_op, reads=["attnT", "consts"] + wreads,
                  writes=["Cpsw"])
            S.add("vector",
                  lambda v, ti=ti: v.tensor_add(xo[ti % 2][:, :], psw[:, :],
                                                xo[ti % 2][:, :]),
                  reads=["Cpsw", f"Cxo{ti % 2}"],
                  writes=["Chres", f"Cxo{ti % 2}"])

            if self.debug:
                S.add("sync",
                      lambda sy, ti=ti: sy.dma_start(
                          self.hres_d[128 * ti:128 * (ti + 1), :], xo[ti % 2][:, :]),
                      reads=["Chres"], writes=["dbg"], dma=True)

            def stats(v, ti=ti):
                v.reduce_sum(s1[:, :], xo[ti % 2][:, :], axis=AX.X)
                v.tensor_copy(self.gapb[:, 2:3], self.gapb[:, 0:1])
                v.tensor_copy(self.gapb[:, 3:4], self.gapb[:, 0:1])
                return v.tensor_scalar_mul(mean[:, :], s1[:, :], 1.0 / D)
            S.add("vector", stats, reads=["Chres"], writes=["Cmean"])
            def sqsum(v, ti=ti):
                v.tensor_tensor(psw[:, :], xo[ti % 2][:, :], xo[ti % 2][:, :], ALU.mult)
                return v.reduce_sum(ssq[:, :], psw[:, :], axis=AX.X)
            S.add("vector", sqsum, reads=["Chres"], writes=["Cssq", "Cpsw"])

            def varop(v):
                v.tensor_tensor(m2[:, :], mean[:, :], mean[:, :], ALU.mult)
                v.tensor_copy(self.gapb[:, 2:3], self.gapb[:, 0:1])
                v.tensor_copy(self.gapb[:, 3:4], self.gapb[:, 0:1])
                return v.scalar_tensor_tensor(var[:, :], ssq[:, :], 1.0 / D,
                                              m2[:, :], ALU.mult, ALU.subtract)
            S.add("vector", varop, reads=["Cmean", "Cssq"], writes=["Cvar"])
            S.add("scalar",
                  lambda s: s.activation(sd[:, :], var[:, :], AF.Sqrt,
                                         bias=self.cbias[:, 1:2]),
                  reads=["Cvar"], writes=["Csd"])
            if self.debug:
                def dmp(v, ti=ti):
                    v.tensor_copy(gt[:, 0:1], mean[:, :])
                    v.tensor_copy(gt[:, 1:2], ssq[:, :])
                    v.tensor_copy(gt[:, 2:3], var[:, :])
                    return v.tensor_copy(gt[:, 3:4], sd[:, :])
                S.add("vector", dmp, reads=["Cmean", "Cssq", "Cvar", "Csd"],
                      writes=["Cgt"])
                S.add("sync",
                      lambda sy, ti=ti: sy.dma_start(
                          self.st_d[128 * ti:128 * (ti + 1), :], gt[:, :]),
                      reads=["Cgt"], writes=["dbg"], dma=True)

            def norm(v, ti=ti):
                v.reciprocal(rstd[:, :], sd[:, :])
                v.tensor_copy(self.gapb[:, 2:3], self.gapb[:, 0:1])
                v.tensor_copy(self.gapb[:, 3:4], self.gapb[:, 0:1])
                return v.tensor_scalar(x1f[ti % 2][:, :], xo[ti % 2][:, :],
                                       mean[:, :], rstd[:, :],
                                       ALU.subtract, ALU.mult)
            S.add("vector", norm, reads=["Csd", "Chres", "Cmean"],
                  writes=[f"Cx1f{ti % 2}", "Chres"])
            S.add("scalar",
                  lambda s, ti=ti: s.activation(x1b[:, :], x1f[ti % 2][:, :],
                                                AF.Copy),
                  reads=[f"Cx1f{ti % 2}"], writes=["Cx1b"])
            S.add("vector",
                  lambda v, ti=ti: v.tensor_sub(x1lo[:, :], x1f[ti % 2][:, :],
                                                x1b[:, :]),
                  reads=[f"Cx1f{ti % 2}", "Cx1b"], writes=["Cx1lo"])
            S.add("sync",
                  lambda sy, ti=ti: sy.dma_start(
                      self.x1_s[128 * ti:128 * (ti + 1), :],
                      x1f[ti % 2][:, :]),
                  reads=[f"Cx1f{ti % 2}"], writes=["x1_s"], dma=True)
            if C_MIN:
                continue
            for b4 in range(4):
                ph = f"Cpst{b4 % 2}"

                def tr(t, b4=b4):
                    return _tr4(t, pst, 1024 * (b4 % 2),
                                [x1b[:, 128 * (4 * b4 + k):
                                      128 * (4 * b4 + k + 1)]
                                 for k in range(4)], self.ident_b[:, :])
                S.add("tensor", tr, reads=["Cx1b", "consts"], writes=[ph])
                S.add("vector",
                      lambda v, ti=ti, b4=b4: v.tensor_copy(
                          x1T[ti % 2][:, 512 * b4:512 * (b4 + 1)],
                          pst[:, 1024 * (b4 % 2):1024 * (b4 % 2) + 512]),
                      reads=[ph], writes=[f"Cx1T{ti % 2}b{b4}"])
            treads = [f"Cx1T{ti % 2}b{b}" for b in range(4)]
            S.add("sync",
                  lambda sy, ti=ti: sy.dma_start(
                      _ap(self.x1T_s, 128 * ti,
                          [[NDC * 2048, 128], [2048, NDC], [1, 128]]),
                      x1T[ti % 2][:, :]),
                  reads=treads, writes=["x1T_s"], dma=True)

            for b4 in range(4):
                ph = f"Cpstl{b4 % 2}"
                S.add("tensor",
                      lambda t, b4=b4: _tr4(
                          t, pst, 1024 * (b4 % 2),
                          [x1lo[:, 128 * (4 * b4 + k):128 * (4 * b4 + k + 1)]
                           for k in range(4)], self.ident_b[:, :]),
                      reads=["Cx1lo", "consts"], writes=[ph])
                S.add("vector",
                      lambda v, b4=b4: v.tensor_copy(
                          x1Tl[:, 512 * b4:512 * (b4 + 1)],
                          pst[:, 1024 * (b4 % 2):1024 * (b4 % 2) + 512]),
                      reads=[ph], writes=[f"Cx1Tlb{b4}"])
            tlreads = [f"Cx1Tlb{b}" for b in range(4)]

            def gate_mm(t, ti=ti):
                terms = [(x1T[ti % 2], gwb), (x1T[ti % 2], gwlo), (x1Tl, gwb)]
                n = len(terms) * NDC
                k = 0
                for xt, gw in terms:
                    for dc in range(NDC):
                        mm = t.matmul(psg[:, 0:E],
                                      xt[:, 128 * dc:128 * (dc + 1)],
                                      gw[:, E * dc:E * (dc + 1)],
                                      start=(k == 0), stop=(k == n - 1))
                        k += 1
                return mm
            S.add("tensor", gate_mm, reads=treads + tlreads + ["Cgw"],
                  writes=["Cpsg"])

            def gate1(v):
                g = lambda: v.tensor_copy(self.gapb[:, 2:3], self.gapb[:, 0:1])
                v.tensor_copy(gt[:, :], psg[:, 0:E])
                g(); g()
                v.reduce_max(gm1[:, :], gt[:, :], axis=AX.X)
                g(); g()
                v.tensor_scalar(gd[:, :], gt[:, :], gm1[:, :], None,
                                ALU.subtract)
                g(); g()
                v.tensor_single_scalar(geq1[:, :], gd[:, :], 0.0, ALU.is_equal)
                g(); g()
                v.scalar_tensor_tensor(gd[:, :], geq1[:, :], NEG, gt[:, :],
                                       ALU.mult, ALU.add)
                g(); g()
                v.reduce_max(gm2[:, :], gd[:, :], axis=AX.X)
                g(); g()
                v.tensor_scalar(gd[:, :], gd[:, :], gm2[:, :], None,
                                ALU.subtract)
                g(); g()
                v.tensor_single_scalar(geq2[:, :], gd[:, :], 0.0, ALU.is_equal)
                g(); g()
                return v.tensor_sub(gdm[:, :], gm1[:, :], gm2[:, :])
            S.add("vector", gate1, reads=["Cpsg"], writes=["Cg1", "Cpsg"])

            def gate_sig(s):
                s.activation(gg1[:, :], gdm[:, :], AF.Sigmoid)
                return s.activation(gg2[:, :], gdm[:, :], AF.Sigmoid,
                                    scale=-1.0)
            S.add("scalar", gate_sig, reads=["Cg1"], writes=["Cgg"])

            def gate2(v, ti=ti):
                v.tensor_scalar(geq1[:, :], geq1[:, :], gg1[:, :], None,
                                ALU.mult)
                v.tensor_copy(self.gapb[:, 2:3], self.gapb[:, 0:1])
                v.tensor_copy(self.gapb[:, 3:4], self.gapb[:, 0:1])
                return v.scalar_tensor_tensor(
                    self.wf_sb[:, E * ti:E * (ti + 1)], geq2[:, :], gg2[:, :],
                    geq1[:, :], ALU.mult, ALU.add)
            S.add("vector", gate2, reads=["Cgg", "Cg1"], writes=["wf"])
            S.add("tensor",
                  lambda t, ti=ti: t.transpose(
                      pwt[0:E, 0:128],
                      _ap(self.wf_sb, E * ti, [[16 * E, 128], [1, E]]),
                      self.ident_f[:, :]),
                  reads=["wf", "consts"], writes=["Cpwt"])
            S.add("vector",
                  lambda v, ti=ti: v.tensor_copy(
                      self.wT_sb[0:E, 128 * ti:128 * (ti + 1)], pwt[0:E, :]),
                  reads=["Cpwt"], writes=["wT"])
        S.emit("c")

    def d_f(self, st):
        S = Sched(self.nc, self.sems)
        x1T = self.sb(st, "F_x1T", [128, NDC * 512], BF16)
        a1b = self.sb(st, "F_a1b", [128, NDC * E * 16], BF16)
        a3b = self.sb(st, "F_a3b", [128, NDC * E * 16], BF16)
        a2b = self.sb(st, "F_a2b", [128, E * NFC * 16], BF16)
        b1e = self.sb(st, "F_b1e", [16, FF], BF16)
        b3e = self.sb(st, "F_b3e", [16, FF], BF16)
        b2e = self.sb(st, "F_b2e", [16, D], BF16)
        wstr = [self.sb(st, f"F_wstr{i}", [128, 2048], BF16) for i in range(2)]
        bstg = [self.sb(st, f"F_bstg{i}", [128, 512], BF16) for i in range(2)]
        bstr = [self.sb(st, f"F_bstr{i}", [128, 1024], BF16) for i in range(2)]
        u1s = self.sb(st, "F_u1s", [16, 512], BF16)
        u3s = self.sb(st, "F_u3s", [16, 512], BF16)
        u2s = self.sb(st, "F_u2s", [16, 512], BF16)
        h1b = [self.sb(st, f"F_h1b{i}", [128, 1024], BF16) for i in range(2)]
        sil = [self.sb(st, f"F_sil{i}", [128, 1024], BF16) for i in range(2)]
        h3b = [self.sb(st, f"F_h3b{i}", [128, 1024], BF16) for i in range(2)]
        a_sb = self.sb(st, "F_a", [128, NFC * 512], BF16)
        acc = self.sb(st, "F_acc", [128, NDC * 512], F32)
        wrow = self.sb(st, "F_wrow", [1, 512], F32)
        whi = self.sb(st, "F_whi", [1, 512], BF16)
        wlo_f = self.sb(st, "F_wlo_f", [1, 512], F32)
        wlo = self.sb(st, "F_wlo", [1, 512], BF16)
        wbc = self.sb(st, "F_wbc", [128, 512], F32)
        tmp = self.sb(st, "F_tmp", [128, 1024], F32)
        x1t = self.sb(st, "F_x1t", [128, D], F32)
        hres = self.sb(st, "F_hres", [128, D], F32)
        yt = hres
        s1 = self.sb(st, "F_s1", [128, 1], F32)
        mean = self.sb(st, "F_mean", [128, 1], F32)
        m2 = self.sb(st, "F_m2", [128, 1], F32)
        ssq = self.sb(st, "F_ssq", [128, 1], F32)
        var = self.sb(st, "F_var", [128, 1], F32)
        sd = self.sb(st, "F_sd", [128, 1], F32)
        rstd = self.sb(st, "F_rstd", [128, 1], F32)
        psA = self.psum(st, "F_psA", [128, 4096], F32)
        PSB = [f"psb{i}" for i in range(8)]

        def bank(i, rows=128):
            return psA[0:rows, 512 * i:512 * (i + 1)]

        for src, dst, nm in [(self.a1c, a1b, "Fa1"), (self.a3c, a3b, "Fa3"),
                             (self.a2c, a2b, "Fa2")]:
            S.add("sync", lambda sy, src=src, dst=dst: sy.dma_start(
                dst[:, :], src[:, :]), reads=["smallc"], writes=[nm], dma=True)
        smreads = ["Fa1", "Fa3", "Fa2"]

        for stp in range(4):
            S.add("sync",
                  lambda sy, stp=stp: sy.dma_start(
                      x1T[:, :],
                      _ap(self.x1T_s, 512 * stp,
                          [[NDC * 2048, 128], [2048, NDC], [1, 512]])),
                  reads=["x1T_s"], writes=["Fx1T"], dma=True)
            for m in range(2):
                wsrc = self.w1c if m == 0 else self.w3c
                bdst = self.b1_s if m == 0 else self.b3_s
                bres = "Fb1_s" if m == 0 else "Fb3_s"
                for fc in range(NFC):
                    k = m * NFC + fc
                    S.add("sync",
                          lambda sy, wsrc=wsrc, fc=fc, k=k: sy.dma_start(
                              wstr[k % 2][:, :],
                              _ap(wsrc, 128 * fc,
                                  [[FF, 128], [128 * FF, NDC], [1, 128]])),
                          reads=["wXc"], writes=[f"Fws{k % 2}"], dma=True)

                    def base_mm(t, fc=fc, k=k):
                        bk = bank(k % 2)
                        for dc in range(NDC):
                            mm = t.matmul(bk,
                                          wstr[k % 2][:, 128 * dc:128 * (dc + 1)],
                                          x1T[:, 512 * dc:512 * (dc + 1)],
                                          start=(dc == 0), stop=(dc == NDC - 1))
                        return mm
                    S.add("tensor", base_mm, reads=[f"Fws{k % 2}", "Fx1T"],
                          writes=[PSB[k % 2], f"Fws{k % 2}"])
                    S.add("scalar",
                          lambda s, k=k: s.activation(bstg[k % 2][:, :],
                                                      bank(k % 2), AF.Copy),
                          reads=[PSB[k % 2]],
                          writes=[f"Fbe{k % 2}", PSB[k % 2]])
                    S.add("sync",
                          lambda sy, bdst=bdst, fc=fc, k=k: sy.dma_start(
                              bdst[128 * fc:128 * (fc + 1), :],
                              bstg[k % 2][:, :]),
                          reads=[f"Fbe{k % 2}"], writes=[bres], dma=True)
            for e in range(E):
                S.add("sync",
                      lambda sy, e=e: sy.dma_start(
                          b1e[:, :], self.b1c[:, FF * e:FF * (e + 1)]),
                      reads=["smallc"], writes=["Fb1e"], dma=True)
                S.add("sync",
                      lambda sy, e=e: sy.dma_start(
                          b3e[:, :], self.b3c[:, FF * e:FF * (e + 1)]),
                      reads=["smallc"], writes=["Fb3e"], dma=True)
                S.add("sync",
                      lambda sy, e=e: sy.dma_start(
                          b2e[:, :], self.b2c[:, D * e:D * (e + 1)]),
                      reads=["smallc"], writes=["Fb2e"], dma=True)
                for uu, (ab, us, bk_i, nm) in enumerate(
                        [(a1b, u1s, 4, "Fu1"), (a3b, u3s, 5, "Fu3")]):
                    def u_mm(t, e=e, ab=ab, bk_i=bk_i):
                        bk = bank(bk_i, 16)
                        for dc in range(NDC):
                            mm = t.matmul(bk,
                                          ab[:, (e * NDC + dc) * 16:
                                             (e * NDC + dc) * 16 + 16],
                                          x1T[:, 512 * dc:512 * (dc + 1)],
                                          start=(dc == 0), stop=(dc == NDC - 1))
                        return mm
                    S.add("tensor", u_mm, reads=["Fx1T"] + smreads,
                          writes=[PSB[bk_i]])
                    S.add("scalar",
                          lambda s, us=us, bk_i=bk_i: s.activation(
                              us[:, :], bank(bk_i, 16), AF.Copy, scale=2.0),
                          reads=[PSB[bk_i]], writes=[nm, PSB[bk_i]])
                for fp in range(NFC // 2):
                    par = fp % 2
                    S.add("sync",
                          lambda sy, fp=fp, par=par: sy.dma_start(
                              bstr[par][:, :],
                              _ap(self.b1_s, 256 * fp * 512,
                                  [[512, 128], [128 * 512, 2], [1, 512]])),
                          reads=["Fb1_s"], writes=[f"Fbr{par}"], dma=True)

                    def h1_mm(t, e=e, fp=fp, par=par):
                        for hh in range(2):
                            fc = 2 * fp + hh
                            mm = t.matmul(
                                psA[:, 1024 * par + 512 * hh:
                                    1024 * par + 512 * (hh + 1)],
                                b1e[:, 128 * fc:128 * (fc + 1)],
                                u1s[:, :], start=True, stop=True)
                        return mm
                    S.add("tensor", h1_mm, reads=["Fu1", "Fb1e"],
                          writes=[PSB[2 * par], PSB[2 * par + 1]])
                    S.add("vector",
                          lambda v, par=par: v.tensor_add(
                              h1b[par][:, :], psA[:, 1024 * par:1024 * (par + 1)],
                              bstr[par][:, :]),
                          reads=[PSB[2 * par], PSB[2 * par + 1], f"Fbr{par}"],
                          writes=[f"Fh1{par}", PSB[2 * par], PSB[2 * par + 1]])
                    S.add("scalar",
                          lambda s, par=par: s.activation(
                              sil[par][:, :], h1b[par][:, :], AF.Silu),
                          reads=[f"Fh1{par}"], writes=[f"Fsil{par}"])
                    S.add("sync",
                          lambda sy, fp=fp, par=par: sy.dma_start(
                              bstr[par][:, :],
                              _ap(self.b3_s, 256 * fp * 512,
                                  [[512, 128], [128 * 512, 2], [1, 512]])),
                          reads=["Fb3_s"], writes=[f"Fbr{par}"], dma=True)

                    def h3_mm(t, e=e, fp=fp, par=par):
                        for hh in range(2):
                            fc = 2 * fp + hh
                            mm = t.matmul(
                                psA[:, 2048 + 1024 * par + 512 * hh:
                                    2048 + 1024 * par + 512 * (hh + 1)],
                                b3e[:, 128 * fc:128 * (fc + 1)],
                                u3s[:, :], start=True, stop=True)
                        return mm
                    S.add("tensor", h3_mm, reads=["Fu3", "Fb3e"],
                          writes=[PSB[4 + 2 * par], PSB[5 + 2 * par]])
                    S.add("vector",
                          lambda v, par=par: v.tensor_add(
                              h3b[par][:, :],
                              psA[:, 2048 + 1024 * par:2048 + 1024 * (par + 1)],
                              bstr[par][:, :]),
                          reads=[PSB[4 + 2 * par], PSB[5 + 2 * par],
                                 f"Fbr{par}"],
                          writes=[f"Fh3{par}", PSB[4 + 2 * par],
                                  PSB[5 + 2 * par]])
                    S.add("gpsimd",
                          lambda g, fp=fp, par=par: g.tensor_mul(
                              a_sb[:, 1024 * fp:1024 * (fp + 1)],
                              sil[par][:, :], h3b[par][:, :]),
                          reads=[f"Fsil{par}", f"Fh3{par}"],
                          writes=[f"Fa{fp}", f"Fsil{par}", f"Fh3{par}"])
                areads = [f"Fa{fp}" for fp in range(NFC // 2)]
                S.add("sync",
                      lambda sy, stp=stp, e=e: sy.dma_start(
                          wrow[:, :],
                          _ap(self.wT_sb, 2048 * e + 512 * stp,
                              [[2048, 1], [1, 512]])),
                      reads=["wT"], writes=["Fwrow"], dma=True)

                def wsplit(v):
                    v.tensor_copy(whi[:, :], wrow[:, :])
                    v.tensor_copy(self.gapb[:, 2:3], self.gapb[:, 0:1])
                    v.tensor_copy(self.gapb[:, 3:4], self.gapb[:, 0:1])
                    v.tensor_sub(wlo_f[:, :], wrow[:, :], whi[:, :])
                    v.tensor_copy(self.gapb[:, 2:3], self.gapb[:, 0:1])
                    v.tensor_copy(self.gapb[:, 3:4], self.gapb[:, 0:1])
                    return v.tensor_copy(wlo[:, :], wlo_f[:, :])
                S.add("vector", wsplit, reads=["Fwrow"], writes=["Fwhl"])

                def wbc_mm(t):
                    t.matmul(bank(7), self.ones_b[:, 0:128], whi[:, :],
                             start=True, stop=False)
                    return t.matmul(bank(7), self.ones_b[:, 0:128], wlo[:, :],
                                    start=False, stop=True)
                S.add("tensor", wbc_mm, reads=["Fwhl", "consts"],
                      writes=[PSB[7]])
                S.add("vector",
                      lambda v: v.tensor_copy(wbc[:, :], bank(7)),
                      reads=[PSB[7]], writes=["Fwbc", PSB[7]])
                def u2_mm(t, e=e):
                    bk = bank(6, 16)
                    for fc in range(NFC):
                        mm = t.matmul(bk,
                                      a2b[:, (e * NFC + fc) * 16:
                                          (e * NFC + fc) * 16 + 16],
                                      a_sb[:, 512 * fc:512 * (fc + 1)],
                                      start=(fc == 0), stop=(fc == NFC - 1))
                    return mm
                S.add("tensor", u2_mm, reads=areads + smreads,
                      writes=[PSB[6]])
                S.add("scalar",
                      lambda s: s.activation(u2s[:, :], bank(6, 16),
                                             AF.Copy, scale=2.0),
                      reads=[PSB[6]], writes=["Fu2", PSB[6]])
                for half in range(2):
                    for fc in range(NFC):
                        k2 = half * NFC + fc
                        S.add("sync",
                              lambda sy, fc=fc, k2=k2: sy.dma_start(
                                  wstr[k2 % 2][:, :],
                                  self.w2c[128 * fc:128 * (fc + 1), :]),
                              reads=["wXc"], writes=[f"Fws{k2 % 2}"], dma=True)

                        def w2_mm(t, half=half, fc=fc, k2=k2):
                            for dc in range(NDC):
                                mm = t.matmul(
                                    psA[:, 256 * dc:256 * (dc + 1)],
                                    wstr[k2 % 2][:, 128 * dc:128 * (dc + 1)],
                                    a_sb[:, 512 * fc + 256 * half:
                                         512 * fc + 256 * (half + 1)],
                                    start=(fc == 0 and dc % 2 == 0),
                                    stop=False)
                            return mm
                        S.add("tensor", w2_mm,
                              reads=[f"Fws{k2 % 2}"] + areads,
                              writes=PSB + [f"Fws{k2 % 2}"])

                    def b2_mm(t, e=e, half=half):
                        for dc in range(NDC):
                            mm = t.matmul(
                                psA[:, 256 * dc:256 * (dc + 1)],
                                b2e[:, 128 * dc:128 * (dc + 1)],
                                u2s[:, 256 * half:256 * (half + 1)],
                                start=False, stop=(dc % 2 == 1))
                        return mm
                    S.add("tensor", b2_mm, reads=["Fu2", "Fb2e"], writes=PSB)
                    for q in range(4):
                        def ev_mult(v, half=half, q=q):
                            for dk in range(4):
                                dc = 4 * q + dk
                                mm = v.tensor_tensor(
                                    _ap(tmp, 256 * dk, [[1024, 128], [1, 256]]),
                                    psA[:, 256 * dc:256 * (dc + 1)],
                                    wbc[:, 256 * half:256 * (half + 1)],
                                    ALU.mult)
                            return mm
                        S.add("vector", ev_mult,
                              reads=[PSB[2 * q], PSB[2 * q + 1], "Fwbc"],
                              writes=["Ftmp"])

                        def ev_add(g, stp=stp, e=e, half=half, q=q):
                            for dk in range(4):
                                dc = 4 * q + dk
                                dst = acc[:, 512 * dc + 256 * half:
                                          512 * dc + 256 * (half + 1)]
                                src = _ap(tmp, 256 * dk, [[1024, 128], [1, 256]])
                                if e == 0:
                                    mm = g.tensor_copy(dst, src)
                                else:
                                    mm = g.tensor_add(dst, dst, src)
                            return mm
                        S.add("gpsimd", ev_add, reads=["Ftmp"],
                              writes=[f"Facc{half}q{q}", "Ftmp",
                                      PSB[2 * q], PSB[2 * q + 1]])
            accreads = [f"Facc{h}q{q}" for h in range(2) for q in range(4)]
            if self.debug:
                S.add("sync",
                      lambda sy, stp=stp: sy.dma_start(
                          self.acc_d[128 * stp:128 * (stp + 1), :], acc[:, :]),
                      reads=accreads, writes=["dbg"], dma=True)
                if stp == 3:
                    S.add("sync",
                          lambda sy: sy.dma_start(self.a_d[:, :], a_sb[:, :]),
                          reads=[f"Fa{fp}" for fp in range(NFC // 2)],
                          writes=["dbg"], dma=True)

                    def udmp(v):
                        v.tensor_copy(self.junkrow[0:16, 0:512], u1s[:, :])
                        v.tensor_copy(self.junkrow[0:16, 512:1024], u3s[:, :])
                        return v.tensor_copy(self.junkrow[0:16, 1024:1536],
                                             u2s[:, :])
                    S.add("vector", udmp, reads=["Fu1", "Fu3", "Fu2"],
                          writes=["dbgu"])
                    S.add("sync",
                          lambda sy: sy.dma_start(
                              _ap(self.u_d, 0, [[1536, 16], [1, 1536]]),
                              self.junkrow[0:16, 0:1536]),
                          reads=["dbgu"], writes=["dbg"], dma=True)
            for sub in range(4):
                ti = 4 * stp + sub
                S.add("sync",
                      lambda sy, ti=ti: sy.dma_start(
                          x1t[:, :], self.x1_s[128 * ti:128 * (ti + 1), :]),
                      reads=["x1_s"], writes=["Fx1t"], dma=True)
                for b4 in range(4):
                    ph = f"Fpt{b4 % 2}"
                    pbs = [PSB[4 + 2 * (b4 % 2)], PSB[5 + 2 * (b4 % 2)]]

                    def trF(t, sub=sub, b4=b4):
                        base = 2048 + 1024 * (b4 % 2)
                        return _tr4(t, psA, base,
                                    [acc[:, 512 * (4 * b4 + k) + 128 * sub:
                                         512 * (4 * b4 + k) + 128 * (sub + 1)]
                                     for k in range(4)], self.ident_f[:, :])
                    S.add("tensor", trF, reads=accreads + ["consts"],
                          writes=[ph] + pbs)
                    S.add("vector",
                          lambda v, sub=sub, b4=b4: v.tensor_add(
                              hres[:, 512 * b4:512 * (b4 + 1)],
                              _ap(psA, 2048 + 1024 * (b4 % 2),
                                  [[4096, 128], [1, 512]]),
                              x1t[:, 512 * b4:512 * (b4 + 1)]),
                          reads=[ph, "Fx1t"], writes=[f"Fhr{b4}"] + pbs)
                hreads = [f"Fhr{b}" for b in range(4)]

                def stats(v):
                    v.reduce_sum(s1[:, :], hres[:, :], axis=AX.X)
                    v.tensor_copy(self.gapb[:, 2:3], self.gapb[:, 0:1])
                    v.tensor_copy(self.gapb[:, 3:4], self.gapb[:, 0:1])
                    return v.tensor_scalar_mul(mean[:, :], s1[:, :], 1.0 / D)
                S.add("vector", stats, reads=hreads, writes=["Fmean"])
                def sqsum(v):
                    v.tensor_tensor(psA[:, 0:2048], hres[:, :], hres[:, :],
                                    ALU.mult)
                    return v.reduce_sum(ssq[:, :], psA[:, 0:2048], axis=AX.X)
                S.add("vector", sqsum, reads=hreads,
                      writes=["Fssq"] + PSB[0:4])

                def varop(v):
                    v.tensor_tensor(m2[:, :], mean[:, :], mean[:, :], ALU.mult)
                    v.tensor_copy(self.gapb[:, 2:3], self.gapb[:, 0:1])
                    v.tensor_copy(self.gapb[:, 3:4], self.gapb[:, 0:1])
                    return v.scalar_tensor_tensor(var[:, :], ssq[:, :],
                                                  1.0 / D, m2[:, :],
                                                  ALU.mult, ALU.subtract)
                S.add("vector", varop, reads=["Fmean", "Fssq"],
                      writes=["Fvar"])
                S.add("scalar",
                      lambda s: s.activation(sd[:, :], var[:, :], AF.Sqrt,
                                             bias=self.cbias[:, 1:2]),
                      reads=["Fvar"], writes=["Fsd"])

                def norm(v):
                    v.reciprocal(rstd[:, :], sd[:, :])
                    v.tensor_copy(self.gapb[:, 2:3], self.gapb[:, 0:1])
                    v.tensor_copy(self.gapb[:, 3:4], self.gapb[:, 0:1])
                    return v.tensor_scalar(yt[:, :], hres[:, :], mean[:, :],
                                           rstd[:, :], ALU.subtract, ALU.mult)
                S.add("vector", norm, reads=["Fsd", "Fmean"] + hreads,
                      writes=["Fyt"] + hreads)
                S.add("sync",
                      lambda sy, ti=ti: sy.dma_start(
                          self.y[128 * ti:128 * (ti + 1), :], yt[:, :]),
                      reads=["Fyt"], writes=["ydone"], dma=True)
        S.emit("f")


def _host_inputs(inputs):
    x = np.ascontiguousarray(np.asarray(inputs["L0_attn_x"], np.float32))
    base = {
        "wq": np.ascontiguousarray(np.asarray(inputs["L0_attn_Wq"], np.float32)),
        "wk": np.ascontiguousarray(np.asarray(inputs["L0_attn_Wk"], np.float32)),
        "wv": np.ascontiguousarray(np.asarray(inputs["L0_attn_Wv"], np.float32)),
        "wo": np.ascontiguousarray(np.asarray(inputs["L0_attn_Wo"], np.float32)),
        "bq": np.asarray(inputs["L0_attn_bq"], np.float32).reshape(1, D),
        "bk": np.asarray(inputs["L0_attn_bk"], np.float32).reshape(1, D),
        "bv": np.asarray(inputs["L0_attn_bv"], np.float32).reshape(1, D),
        "bo": np.asarray(inputs["L0_attn_bo"], np.float32).reshape(1, D),
        "gwt": np.ascontiguousarray(np.asarray(inputs["L0_ffn_gate_w"],
                                               np.float32).T),
        "w1t": np.ascontiguousarray(np.asarray(inputs["L0_ffn_w1"],
                                               np.float32).T),
        "w3t": np.ascontiguousarray(np.asarray(inputs["L0_ffn_w3"],
                                               np.float32).T),
        "w2t": np.ascontiguousarray(np.asarray(inputs["L0_ffn_w2"],
                                               np.float32).T),
        "a1h": np.ascontiguousarray(
            np.asarray(inputs["L0_ffn_a1"], np.float32)
            .reshape(E, 16, NDC, 128).transpose(3, 0, 2, 1).reshape(128, -1)),
        "a3h": np.ascontiguousarray(
            np.asarray(inputs["L0_ffn_a3"], np.float32)
            .reshape(E, 16, NDC, 128).transpose(3, 0, 2, 1).reshape(128, -1)),
        "a2h": np.ascontiguousarray(
            np.asarray(inputs["L0_ffn_a2"], np.float32)
            .reshape(E, 16, NFC, 128).transpose(3, 0, 2, 1).reshape(128, -1)),
        "b1h": np.ascontiguousarray(
            np.asarray(inputs["L0_ffn_b1"], np.float32)
            .transpose(2, 0, 1).reshape(16, -1)),
        "b3h": np.ascontiguousarray(
            np.asarray(inputs["L0_ffn_b3"], np.float32)
            .transpose(2, 0, 1).reshape(16, -1)),
        "b2h": np.ascontiguousarray(
            np.asarray(inputs["L0_ffn_b2"], np.float32)
            .transpose(2, 0, 1).reshape(16, -1)),
    }
    uu = np.arange(128)[:, None]
    ww = np.arange(MASKW)[None, :]
    maps = []
    for c in range(8):
        b, j = c // 4, c % 4
        qrows = np.array([hh * 512 + 32 * (4 * s + j) + r
                          for hh in range(H) for s in range(4)
                          for r in range(32)])
        orows = np.concatenate([np.arange(512 * (4 * s + j),
                                          512 * (4 * s + j) + 512)
                                for s in range(4)])
        m = dict(base)
        m["x_kv"] = x[b]
        m["xq"] = np.ascontiguousarray(x[b][qrows])
        m["x_own"] = np.ascontiguousarray(x[b][orows])
        m["strip"] = np.where(uu <= ww - 1920 + 512 * j, np.float32(0),
                              np.float32(NEG)).astype(np.float32)
        maps.append(m)
    return maps


def kernel(**inputs):
    maps = _host_inputs(inputs)
    nc = Build().build()
    res = run_bass_kernel_spmd(nc, maps, core_ids=list(range(8)))
    out = np.empty((2, T, D), np.float32)
    for c in range(8):
        b, j = c // 4, c % 4
        y = res.results[c]["y"]
        for s in range(4):
            out[b, 512 * (4 * s + j):512 * (4 * s + j) + 512] = \
                y[512 * s:512 * (s + 1)]
    return out


# version 03 (final)
# speedup vs baseline: 1.0495x; 1.0495x over previous
import bisect
import numpy as np
from contextlib import ExitStack

import concourse.bass as bass
import concourse.mybir as mybir
from concourse.bass_utils import run_bass_kernel_spmd

F32 = mybir.dt.float32
BF16 = mybir.dt.bfloat16
ALU = mybir.AluOpType
AF = mybir.ActivationFunctionType
AX = mybir.AxisListType

D = 2048
T = 8192
H = 16
FF = 5632
E = 8
NDC = 16
NFC = 44
NST = 16
QSCALE = float(1.0 / np.sqrt(128.0))
EXPOFF = -12.0
MASKW = 2432
NEG = -1e30

ENGS = ("tensor", "vector", "scalar", "gpsimd", "sync")
SMALL_IN_AV = False
C_MIN = False


def _tr4(t, ps, off, slices, ident):
    n = len(slices)
    for k, sl in enumerate(slices):
        mm = t.matmul(ps[:, off + 128 * k:off + 128 * (k + 1)], sl, ident,
                      is_transpose=True, start=(k == 0), stop=(k == n - 1))
    return mm


def _ap(t, off, dims):
    tt = getattr(t, "tensor", None)
    return bass.AP(tt if tt is not None and not callable(tt) else t, off, dims)


class Op:
    __slots__ = ("eng", "fn", "dma", "idx", "deps", "dma_ord")

    def __init__(self, eng, fn, dma):
        self.eng, self.fn, self.dma = eng, fn, dma
        self.deps = {}
        self.dma_ord = 0


class Sched:

    def __init__(self, nc, sems):
        self.nc = nc
        self.sems = sems
        self.ops = {e: [] for e in ENGS}
        self.res = {}

    def add(self, eng, fn, reads=(), writes=(), dma=False):
        op = Op(eng, fn, dma)
        op.idx = len(self.ops[eng])
        if dma:
            op.dma_ord = sum(1 for o in self.ops[eng] if o.dma)
        self.ops[eng].append(op)

        def dep(w):
            if w is None:
                return
            peng, pidx, pdma, pord = w
            if peng == eng and not pdma:
                return
            key = (peng, pdma)
            val = pord if pdma else pidx
            if op.deps.get(key, -1) < val:
                op.deps[key] = val

        for r in reads:
            ent = self.res.get(r)
            if ent is None:
                continue
            dep(ent[0])
            ent[1].append((eng, op.idx, dma, op.dma_ord))
        me = (eng, op.idx, dma, op.dma_ord)
        for w in writes:
            ent = self.res.setdefault(w, [None, []])
            dep(ent[0])
            for rd in ent[1]:
                dep(rd)
            ent[0] = me
            ent[1] = []
        return op

    def emit(self, tag):
        nc = self.nc
        needed = {}
        for c in ENGS:
            for op in self.ops[c]:
                for (p, pdma), val in op.deps.items():
                    if not pdma:
                        needed.setdefault(p, set()).add(val)
        needed = {k: sorted(v) for k, v in needed.items()}
        psem = {p: self.sems(f"{tag}_p{p[:2]}") for p in needed}
        marked = {p: set(v) for p, v in needed.items()}
        dsem = {e: self.sems(f"{tag}_d{e[:2]}")
                for e in ENGS if any(o.dma for o in self.ops[e])}
        ndma = {e: sum(1 for o in self.ops[e] if o.dma) for e in dsem}
        waited = {}

        def emit_eng(eng_obj, e):
            for op in self.ops[e]:
                for (p, pdma), val in sorted(op.deps.items(),
                                             key=lambda kv: str(kv[0])):
                    if pdma:
                        sem, wv = dsem[p], 16 * (val + 1)
                    else:
                        sem = psem[p]
                        wv = bisect.bisect_right(needed[p], val)
                    key = (e, sem.num)
                    if waited.get(key, -1) >= wv:
                        continue
                    waited[key] = wv
                    eng_obj.wait_ge(sem, wv)
                inst = op.fn(eng_obj)
                if op.dma:
                    inst.then_inc(dsem[e], 16)
                elif e in marked and op.idx in marked[e]:
                    eng_obj.drain().then_inc(psem[e], 1)
            if e == "sync":
                for p, sem in dsem.items():
                    eng_obj.wait_ge(sem, 16 * ndma[p])

        with nc.Block() as blk:
            for e in ENGS:
                if self.ops[e] or e == "sync":
                    getattr(blk, e)(lambda eng_obj, e=e: emit_eng(eng_obj, e))
        nc.all_engine_barrier()


class Build:
    def __init__(self, phases=99, debug=False):
        self.phases = phases
        self.debug = debug
        nc = bass.Bass("TRN2", target_bir_lowering=False)
        self.nc = nc
        self.stack = ExitStack()
        self._semn = 0
        dram = lambda n, s, d, k="ExternalInput": nc.dram_tensor(n, s, d, kind=k).ap()
        sk = "ExternalOutput" if debug else "Internal"
        self.x_kv = dram("x_kv", [T, D], F32)
        self.xq = dram("xq", [2048, D], F32)
        self.x_own = dram("x_own", [2048, D], F32)
        self.strip = dram("strip", [128, MASKW], F32)
        for nm, sh in [("wq", [D, D]), ("wk", [D, D]), ("wv", [D, D]),
                       ("wo", [D, D]), ("bq", [1, D]), ("bk", [1, D]),
                       ("bv", [1, D]), ("bo", [1, D]), ("gwt", [D, E]),
                       ("w1t", [D, FF]), ("w3t", [D, FF]), ("w2t", [FF, D]),
                       ("a1h", [128, NDC * E * 16]), ("a3h", [128, NDC * E * 16]),
                       ("a2h", [128, E * NFC * 16]), ("b1h", [16, E * FF]),
                       ("b3h", [16, E * FF]), ("b2h", [16, E * D])]:
            setattr(self, nm, dram(nm, sh, F32))
        self.y = dram("y", [2048, D], F32, "ExternalOutput")
        self.xT_s = dram("xT_s", [128, NDC * T], BF16, sk)
        self.kT_s = dram("kT_s", [128, H * T], BF16, sk)
        self.v_s = dram("v_s", [T, D], BF16, sk)
        self.qT_s = dram("qT_s", [128, H * 2048], BF16, sk)
        self.x1_s = dram("x1_s", [2048, D], F32, sk)
        self.x1T_s = dram("x1T_s", [128, NDC * 2048], BF16, sk)
        self.w1c = dram("w1c", [D, FF], BF16, sk)
        self.w3c = dram("w3c", [D, FF], BF16, sk)
        self.w2c = dram("w2c", [FF, D], BF16, sk)
        self.a1c = dram("a1c", [128, NDC * E * 16], BF16, sk)
        self.a3c = dram("a3c", [128, NDC * E * 16], BF16, sk)
        self.a2c = dram("a2c", [128, E * NFC * 16], BF16, sk)
        self.b1c = dram("b1c", [16, E * FF], BF16, sk)
        self.b3c = dram("b3c", [16, E * FF], BF16, sk)
        self.b2c = dram("b2c", [16, E * D], BF16, sk)
        self.b1_s = dram("b1_s", [FF, 512], BF16, sk)
        self.b3_s = dram("b3_s", [FF, 512], BF16, sk)
        if debug:
            self.attnT_d = dram("attnT_d", [128, H * 2048], BF16, "ExternalOutput")
            self.hres_d = dram("hres_d", [2048, D], F32, "ExternalOutput")
            self.st_d = dram("st_d", [2048, 8], F32, "ExternalOutput")
            self.acc_d = dram("acc_d", [512, NDC * 512], F32, "ExternalOutput")
            self.a_d = dram("a_d", [128, NFC * 512], BF16, "ExternalOutput")
            self.u_d = dram("u_d", [16, 1536], BF16, "ExternalOutput")
            self.wf_d = dram("wf_d", [128, 16 * E], F32, "ExternalOutput")
            self.wT_d = dram("wT_d", [8, 2048], F32, "ExternalOutput")

    def sems(self, name):
        self._semn += 1
        return self.stack.enter_context(
            self.nc.semaphore(f"{name}{self._semn}"))

    def sb(self, st, name, shape, dtype):
        return st.enter_context(self.nc.sbuf_tensor(name, shape, dtype))

    def psum(self, st, name, shape, dtype=F32):
        return st.enter_context(self.nc.psum_tensor(name, shape, dtype))

    def build(self):
        nc = self.nc
        with self.stack:
            ps = self.stack
            self.ident_b = self.sb(ps, "ident_b", [128, 128], BF16)
            self.ident_f = self.sb(ps, "ident_f", [128, 128], F32)
            self.ones_b = self.sb(ps, "ones_b", [1, 512], BF16)
            self.onec_b = self.sb(ps, "onec_b", [128, 1], BF16)
            self.wf_sb = self.sb(ps, "wf_sb", [128, 16 * E], F32)
            self.cbias = self.sb(ps, "cbias", [128, 2], F32)
            self.gapb = self.sb(ps, "gapb", [128, 4], F32)
            self.junkrow = (self.sb(ps, "junkrow", [16, 1536], BF16)
                            if self.debug else None)
            self.wT_sb = self.sb(ps, "wT_sb", [8, 2048], F32)
            s0 = self.stack.enter_context(nc.semaphore("s00"))
            self.first_sem = s0.num
            nc.gpsimd.dma_reset(range(self.first_sem, 256))
            nc.gpsimd.sem_clear(range(self.first_sem, 256))
            nc.all_engine_barrier()
            self.d_consts()
            if self.phases >= 1:
                with ExitStack() as st:
                    self.d_ak(st)
            if self.phases >= 2:
                with ExitStack() as st:
                    self.d_av(st)
            if self.phases >= 3:
                with ExitStack() as st:
                    self.d_aq(st)
            if self.phases >= 4:
                with ExitStack() as bc:
                    attnT = self.sb(bc, "attnT", [128, H * 2048], BF16)
                    with ExitStack() as st:
                        self.d_b(st, attnT)
                    if self.debug:
                        self.d_dump(self.attnT_d, attnT, "dmpB")
                    if self.phases >= 5:
                        with ExitStack() as st:
                            self.d_c(st, attnT)
            if self.phases >= 5 and self.debug:
                self.d_dump(self.wf_d, self.wf_sb, "dmpW")
                self.d_dump(self.wT_d, self.wT_sb, "dmpT")
            if self.phases >= 6:
                with ExitStack() as st:
                    self.d_f(st)
            nc.gpsimd.sem_clear(range(self.first_sem, 256))
            nc.all_engine_barrier()
        return nc

    def d_dump(self, dst, src, tag):
        nc = self.nc
        sd = self.sems(tag)
        with nc.Block() as blk:
            @blk.sync
            def _(sy):
                sy.dma_start(dst[:, :], src[:, :]).then_inc(sd, 16)
                sy.wait_ge(sd, 16)
        nc.all_engine_barrier()

    def d_consts(self):
        S = Sched(self.nc, self.sems)

        def consts(g):
            g.memset(self.ones_b[:, :], 1.0)
            g.memset(self.onec_b[:, :], 1.0)
            for ident in (self.ident_b, self.ident_f):
                g.memset(ident[:, :], 0.0)
                g.affine_select(ident[:, :], ident[:, :], [[-1, 128]],
                                ALU.not_equal, 1.0, base=0,
                                channel_multiplier=1)
            g.memset(self.cbias[:, 0:1], EXPOFF)
            return g.memset(self.cbias[:, 1:2], 1e-5)
        S.add("gpsimd", consts, writes=["consts"])
        S.emit("cn")

    def _castload(self, S, dst_sb, src, n, tag, stg, cols=D):
        nb = 2 if stg[0] is not stg[1] else 1
        for i in range(n):
            par = i % nb
            S.add("sync",
                  lambda sy, i=i, par=par: sy.dma_start(
                      stg[par][:, 0:cols], src[128 * i:128 * (i + 1), :]),
                  writes=[f"{tag}stg{par}"], dma=True)
            S.add("vector",
                  lambda v, i=i, par=par: v.tensor_copy(
                      dst_sb[:, cols * i:cols * (i + 1)], stg[par][:, 0:cols]),
                  reads=[f"{tag}stg{par}"], writes=[f"{tag}w{i}"])

    def _bias(self, S, dst, src, tag, stg):
        S.add("sync",
              lambda sy: sy.dma_start(_ap(stg[0], 0, [[D, 1], [1, D]]),
                                      src[:, :]),
              writes=[f"{tag}stg0"], dma=True)
        S.add("vector", lambda v: v.tensor_copy(dst[:, :], stg[0][0:1, 0:D]),
              reads=[f"{tag}stg0"], writes=[f"{tag}b"])

    def d_ak(self, st):
        S = Sched(self.nc, self.sems)
        w_sb = self.sb(st, "Ak_w", [128, NDC * D], BF16)
        bb = self.sb(st, "Ak_bb", [1, D], BF16)
        stg = [self.sb(st, f"Ak_stg{i}", [128, D], F32) for i in range(2)]
        xst = [self.sb(st, f"Ak_x{i}", [128, 4 * D], BF16) for i in range(2)]
        xT = [self.sb(st, f"Ak_xT{i}", [128, NDC * 512], BF16) for i in range(2)]
        kev = self.sb(st, "Ak_kev", [128, H * 512], BF16)
        pst = self.psum(st, "Ak_pst", [128, 2048], BF16)
        psk = self.psum(st, "Ak_psk", [128, 1024], F32)
        self._castload(S, w_sb, self.wk, NDC, "Ak", stg)
        self._bias(S, bb, self.bk, "Ak", stg)
        wreads = [f"Akw{i}" for i in range(NDC)] + ["Akb"]
        for i in range(NST):
            for sub in range(4):
                n = 4 * i + sub
                S.add("sync",
                      lambda sy, i=i, sub=sub, n=n: sy.dma_start(
                          stg[n % 2][:, :],
                          self.x_kv[512 * i + 128 * sub:
                                    512 * i + 128 * (sub + 1), :]),
                      writes=[f"Akstg{n % 2}"], dma=True)
                S.add("vector",
                      lambda v, i=i, sub=sub, n=n: v.tensor_copy(
                          xst[i % 2][:, D * sub:D * (sub + 1)],
                          stg[n % 2][:, :]),
                      reads=[f"Akstg{n % 2}"], writes=[f"Akx{i % 2}s{sub}"])
            for dc in range(NDC):
                ph = f"Akpst{dc % 2}"

                def tr(t, i=i, dc=dc):
                    return _tr4(t, pst, 1024 * (dc % 2),
                                [xst[i % 2][:, D * sub + 128 * dc:
                                            D * sub + 128 * (dc + 1)]
                                 for sub in range(4)], self.ident_b[:, :])
                S.add("tensor", tr,
                      reads=[f"Akx{i % 2}s{s}" for s in range(4)] + ["consts"],
                      writes=[ph])
                S.add("vector",
                      lambda v, i=i, dc=dc: v.tensor_copy(
                          xT[i % 2][:, 512 * dc:512 * (dc + 1)],
                          pst[:, 1024 * (dc % 2):1024 * (dc % 2) + 512]),
                      reads=[ph], writes=[f"AkxT{i % 2}c{dc}"])
            xreads = [f"AkxT{i % 2}c{dc}" for dc in range(NDC)]
            S.add("sync",
                  lambda sy, i=i: sy.dma_start(
                      _ap(self.xT_s, 512 * i,
                          [[NDC * T, 128], [T, NDC], [1, 512]]),
                      xT[i % 2][:, :]),
                  reads=xreads, writes=[f"xT_s{i}"], dma=True)
            for fc in range(H):
                pb = f"Akpsk{fc % 2}"

                def kp(t, i=i, fc=fc):
                    bank = psk[:, 512 * (fc % 2):512 * (fc % 2 + 1)]
                    for dc in range(NDC):
                        t.matmul(bank,
                                 w_sb[:, D * dc + 128 * fc:
                                      D * dc + 128 * (fc + 1)],
                                 xT[i % 2][:, 512 * dc:512 * (dc + 1)],
                                 start=(dc == 0), stop=False)
                    return t.matmul(bank, bb[:, 128 * fc:128 * (fc + 1)],
                                    self.ones_b[:, :], start=False, stop=True)
                S.add("tensor", kp, reads=xreads + wreads + ["consts"],
                      writes=[pb])
                S.add("scalar",
                      lambda s, fc=fc: s.activation(
                          kev[:, 512 * fc:512 * (fc + 1)],
                          psk[:, 512 * (fc % 2):512 * (fc % 2 + 1)], AF.Copy),
                      reads=[pb], writes=[f"Akkev{fc}"])
                S.add("sync",
                      lambda sy, i=i, fc=fc: sy.dma_start(
                          self.kT_s[:, T * fc + 512 * i:T * fc + 512 * (i + 1)],
                          kev[:, 512 * fc:512 * (fc + 1)]),
                      reads=[f"Akkev{fc}"], writes=["kT_s"], dma=True)
        S.emit("ak")

    def d_av(self, st):
        S = Sched(self.nc, self.sems)
        w_sb = self.sb(st, "Av_w", [128, NDC * D], BF16)
        bb = self.sb(st, "Av_bb", [1, D], BF16)
        stg = [self.sb(st, f"Av_stg{i}", [128, D], F32) for i in range(2)]
        cst = self.sb(st, "Av_cst", [128, D], BF16)
        xT = [self.sb(st, f"Av_xT{i}", [128, NDC * 512], BF16) for i in range(2)]
        vev = [self.sb(st, f"Av_vev{i}", [128, 4 * D], BF16) for i in range(2)]
        psv = self.psum(st, "Av_psv", [128, 1024], F32)
        self._castload(S, w_sb, self.wv, NDC, "Av", stg)
        self._bias(S, bb, self.bv, "Av", stg)
        wreads = [f"Avw{i}" for i in range(NDC)] + ["Avb"]
        jobs = []
        for srcn, dstn, rows, tot in [("a1h", "a1c", 128, NDC * E * 16),
                                      ("a3h", "a3c", 128, NDC * E * 16),
                                      ("a2h", "a2c", 128, E * NFC * 16),
                                      ("b1h", "b1c", 16, E * FF),
                                      ("b3h", "b3c", 16, E * FF),
                                      ("b2h", "b2c", 16, E * D)]:
            src, dst = getattr(self, srcn), getattr(self, dstn)
            off = 0
            while off < tot:
                w = min(2048, tot - off)
                jobs.append((src, dst, rows, off, w, tot))
                off += w

        def smalljob(k):
            src, dst, rows, off, w, tot = jobs[k]
            par = k % 2
            S.add("sync",
                  lambda sy: sy.dma_start(
                      _ap(stg[par], 0, [[D, rows], [1, w]]),
                      _ap(src, off, [[tot, rows], [1, w]])),
                  writes=[f"Avstg{par}"], dma=True)
            S.add("vector",
                  lambda v: v.tensor_copy(
                      _ap(cst, 0, [[D, rows], [1, w]]),
                      _ap(stg[par], 0, [[D, rows], [1, w]])),
                  reads=[f"Avstg{par}"], writes=["Avcst"])
            S.add("gpsimd",
                  lambda g: g.dma_start(
                      _ap(dst, off, [[tot, rows], [1, w]]),
                      _ap(cst, 0, [[D, rows], [1, w]])),
                  reads=["Avcst"], writes=["smallc"], dma=True)

        job_k = [0]

        def pump():
            if job_k[0] < len(jobs):
                smalljob(job_k[0])
                job_k[0] += 1

        for i in range(NST):
            S.add("sync",
                  lambda sy, i=i: sy.dma_start(
                      xT[i % 2][:, :],
                      _ap(self.xT_s, 512 * i,
                          [[NDC * T, 128], [T, NDC], [1, 512]])),
                  reads=[f"xT_s{i}"], writes=[f"AvxT{i % 2}"], dma=True)
            for sub in range(4):
                for fc in range(4):
                    g4 = 4 * sub + fc
                    pb = f"Avps{g4 % 2}"

                    def vp(t, i=i, sub=sub, fc=fc, g4=g4):
                        bank = psv[:, 512 * (g4 % 2):512 * (g4 % 2 + 1)]
                        for dc in range(NDC):
                            t.matmul(bank,
                                     xT[i % 2][:, 512 * dc + 128 * sub:
                                               512 * dc + 128 * (sub + 1)],
                                     w_sb[:, D * dc + 512 * fc:
                                          D * dc + 512 * (fc + 1)],
                                     start=(dc == 0), stop=False)
                        return t.matmul(bank, self.ones_b[:, 0:128],
                                        bb[:, 512 * fc:512 * (fc + 1)],
                                        start=False, stop=True)
                    S.add("tensor", vp,
                          reads=[f"AvxT{i % 2}"] + wreads + ["consts"],
                          writes=[pb])
                    S.add("scalar",
                          lambda s, i=i, sub=sub, fc=fc, g4=g4: s.activation(
                              vev[i % 2][:, D * sub + 512 * fc:
                                         D * sub + 512 * (fc + 1)],
                              psv[:, 512 * (g4 % 2):512 * (g4 % 2 + 1)],
                              AF.Copy),
                          reads=[pb], writes=[f"Avev{i % 2}s{sub}f{fc}"])
                S.add("sync",
                      lambda sy, i=i, sub=sub: sy.dma_start(
                          self.v_s[512 * i + 128 * sub:
                                   512 * i + 128 * (sub + 1), :],
                          vev[i % 2][:, D * sub:D * (sub + 1)]),
                      reads=[f"Avev{i % 2}s{sub}f{fc}" for fc in range(4)],
                      writes=["v_s"], dma=True)
        if SMALL_IN_AV:
            while job_k[0] < len(jobs):
                pump()
        S.emit("av")
        if not SMALL_IN_AV:
            S2 = Sched(self.nc, self.sems)
            S_sav, self_jobs = S, jobs
            def smalljob2(S, k):
                src, dst, rows, off, w, tot = jobs[k]
                par = k % 2
                S.add("sync",
                      lambda sy: sy.dma_start(
                          _ap(stg[par], 0, [[D, rows], [1, w]]),
                          _ap(src, off, [[tot, rows], [1, w]])),
                      writes=[f"Avstg{par}"], dma=True)
                S.add("vector",
                      lambda v: v.tensor_copy(
                          _ap(cst, 0, [[D, rows], [1, w]]),
                          _ap(stg[par], 0, [[D, rows], [1, w]])),
                      reads=[f"Avstg{par}"], writes=["Avcst"])
                S.add("gpsimd",
                      lambda g: g.dma_start(
                          _ap(dst, off, [[tot, rows], [1, w]]),
                          _ap(cst, 0, [[D, rows], [1, w]])),
                      reads=["Avcst"], writes=["smallc"], dma=True)
            for k in range(len(jobs)):
                smalljob2(S2, k)
            S2.emit("sm")

    def d_aq(self, st):
        S = Sched(self.nc, self.sems)
        w_sb = self.sb(st, "Aq_w", [128, NDC * D], BF16)
        bb = self.sb(st, "Aq_bb", [1, D], BF16)
        stg = [self.sb(st, f"Aq_stg{i}", [128, D], F32) for i in range(2)]
        xqb = [self.sb(st, f"Aq_x{i}", [128, D], BF16) for i in range(2)]
        xqT = [self.sb(st, f"Aq_xT{i}", [128, D], BF16) for i in range(2)]
        qpe = [self.sb(st, f"Aq_qpe{i}", [128, D], BF16) for i in range(2)]
        qT = [self.sb(st, f"Aq_qT{i}", [128, 2048], BF16) for i in range(2)]
        pst = self.psum(st, "Aq_pst", [128, 2048], BF16)
        psq = self.psum(st, "Aq_psq", [128, 1024], F32)
        self._castload(S, w_sb, self.wq, NDC, "Aq", stg)
        self._bias(S, bb, self.bq, "Aq", stg)
        wreads = [f"Aqw{i}" for i in range(NDC)] + ["Aqb"]
        for h in range(H):
            n = NDC + h
            S.add("sync",
                  lambda sy, h=h, n=n: sy.dma_start(
                      stg[n % 2][:, :], self.xq[128 * h:128 * (h + 1), :]),
                  writes=[f"Aqstg{n % 2}"], dma=True)
            S.add("vector",
                  lambda v, h=h, n=n: v.tensor_copy(xqb[h % 2][:, :],
                                                    stg[n % 2][:, :]),
                  reads=[f"Aqstg{n % 2}"], writes=[f"Aqx{h % 2}"])
            for b4 in range(4):
                ph = f"Aqpst{b4 % 2}"

                def tr(t, h=h, b4=b4):
                    return _tr4(t, pst, 1024 * (b4 % 2),
                                [xqb[h % 2][:, 128 * (4 * b4 + k):
                                            128 * (4 * b4 + k + 1)]
                                 for k in range(4)], self.ident_b[:, :])
                S.add("tensor", tr, reads=[f"Aqx{h % 2}", "consts"],
                      writes=[ph])
                S.add("vector",
                      lambda v, h=h, b4=b4: v.tensor_copy(
                          xqT[h % 2][:, 512 * b4:512 * (b4 + 1)],
                          pst[:, 1024 * (b4 % 2):1024 * (b4 % 2) + 512]),
                      reads=[ph], writes=[f"AqxT{h % 2}b{b4}"])
            xreads = [f"AqxT{h % 2}b{b}" for b in range(4)]
            for fc in range(4):
                pb = f"Aqpsq{fc % 2}"

                def qp(t, h=h, fc=fc):
                    bank = psq[:, 512 * (fc % 2):512 * (fc % 2 + 1)]
                    for dc in range(NDC):
                        t.matmul(bank, xqT[h % 2][:, 128 * dc:128 * (dc + 1)],
                                 w_sb[:, D * dc + 512 * fc:
                                      D * dc + 512 * (fc + 1)],
                                 start=(dc == 0), stop=False)
                    return t.matmul(bank, self.ones_b[:, 0:128],
                                    bb[:, 512 * fc:512 * (fc + 1)],
                                    start=False, stop=True)
                S.add("tensor", qp, reads=xreads + wreads + ["consts"],
                      writes=[pb])
                S.add("scalar",
                      lambda s, h=h, fc=fc: s.activation(
                          qpe[h % 2][:, 512 * fc:512 * (fc + 1)],
                          psq[:, 512 * (fc % 2):512 * (fc % 2 + 1)],
                          AF.Copy, scale=QSCALE),
                      reads=[pb], writes=[f"Aqpe{h % 2}f{fc}"])
            for sub in range(H):
                ph = f"Aqp2_{sub % 2}"
                S.add("tensor",
                      lambda t, h=h, sub=sub: t.transpose(
                          pst[:, 1024 * (sub % 2):1024 * (sub % 2) + 128],
                          qpe[h % 2][:, 128 * sub:128 * (sub + 1)],
                          self.ident_b[:, :]),
                      reads=[f"Aqpe{h % 2}f{sub // 4}", "consts"], writes=[ph])
                S.add("vector",
                      lambda v, h=h, sub=sub: v.tensor_copy(
                          _ap(qT[h % 2], sub, [[2048, 128], [512, 4], [16, 32]]),
                          _ap(pst, 1024 * (sub % 2),
                              [[2048, 128], [32, 4], [1, 32]])),
                      reads=[ph], writes=[f"AqqT{h % 2}s{sub}"])
            S.add("sync",
                  lambda sy, h=h: sy.dma_start(
                      self.qT_s[:, 2048 * h:2048 * (h + 1)], qT[h % 2][:, :]),
                  reads=[f"AqqT{h % 2}s{s}" for s in range(H)],
                  writes=["qT_s"], dma=True)
        S.emit("aq")

    def d_b(self, st, attnT):
        S = Sched(self.nc, self.sems)
        strip_sb = self.sb(st, "strip_sb", [128, MASKW], F32)
        kT = [self.sb(st, f"b_kT{i}", [128, T], BF16) for i in range(2)]
        vb = [self.sb(st, f"b_v{i}", [128, T], BF16) for i in range(2)]
        qT = [self.sb(st, f"b_qT{i}", [128, 2048], BF16) for i in range(2)]
        P = [self.sb(st, f"b_P{i}", [128, 1024], BF16) for i in range(2)]
        lnl = self.sb(st, "b_lnl", [1, 512], F32)
        rrow = self.sb(st, "b_rrow", [1, 512], F32)
        rhi = self.sb(st, "b_rhi", [1, 512], BF16)
        rlo_f = self.sb(st, "b_rlo_f", [1, 512], F32)
        rlo = self.sb(st, "b_rlo", [1, 512], BF16)
        rb = self.sb(st, "b_rb", [128, 512], F32)
        wp_stg = self.sb(st, "wp_stg", [128, FF], F32)
        wp_bf = self.sb(st, "wp_bf", [128, FF], BF16)
        psc = self.psum(st, "b_psc", [128, 2048], F32)
        pav = self.psum(st, "b_pav", [128, 512], F32)
        pl = self.psum(st, "b_pl", [128, 512], F32)
        prb = self.psum(st, "b_prb", [128, 512], F32)

        S.add("sync", lambda sy: sy.dma_start(strip_sb[:, :], self.strip[:, :]),
              writes=["strip"], dma=True)
        jobs = ([(self.w1t, self.w1c, i, FF) for i in range(NDC)]
                + [(self.w3t, self.w3c, i, FF) for i in range(NDC)]
                + [(self.w2t, self.w2c, i, D) for i in range(NFC)])

        def wpjob(k):
            src, dst, i, cols = jobs[k]
            S.add("sync",
                  lambda sy: sy.dma_start(wp_stg[:, 0:cols],
                                          src[128 * i:128 * (i + 1), :]),
                  writes=["wps"], dma=True)
            S.add("vector",
                  lambda v: v.tensor_copy(wp_bf[:, 0:cols], wp_stg[:, 0:cols]),
                  reads=["wps"], writes=["wpb"])
            S.add("sync",
                  lambda sy: sy.dma_start(dst[128 * i:128 * (i + 1), :],
                                          wp_bf[:, 0:cols]),
                  reads=["wpb"], writes=["wXc"], dma=True)

        job_k = [0]

        def pump():
            if job_k[0] < len(jobs):
                wpjob(job_k[0])
                job_k[0] += 1

        for h in range(H):
            S.add("sync",
                  lambda sy, h=h: sy.dma_start(
                      kT[h % 2][:, :], self.kT_s[:, T * h:T * (h + 1)]),
                  reads=["kT_s"], writes=[f"bkT{h % 2}"], dma=True)
            S.add("sync",
                  lambda sy, h=h: sy.dma_start(
                      vb[h % 2][:, :],
                      _ap(self.v_s, 128 * h,
                          [[D, 128], [128 * D, T // 128], [1, 128]])),
                  reads=["v_s"], writes=[f"bv{h % 2}"], dma=True)
            S.add("sync",
                  lambda sy, h=h: sy.dma_start(
                      qT[h % 2][:, :], self.qT_s[:, 2048 * h:2048 * (h + 1)]),
                  reads=["qT_s"], writes=[f"bq{h % 2}"], dma=True)
            for stp in range(4):
                gs = 4 * h + stp
                ncs = 8 * (stp + 1)

                def sc_op(t, h=h, stp=stp, c=0):
                    for i in range(2):
                        k = 2 * c + i
                        mm = t.matmul(
                            psc[:, 1024 * (c % 2) + 512 * i:
                                1024 * (c % 2) + 512 * (i + 1)],
                            kT[h % 2][:, 128 * k:128 * (k + 1)],
                            qT[h % 2][:, 512 * stp:512 * (stp + 1)],
                            start=True, stop=True)
                    return mm

                def mk_op(v, stp=stp, c=0):
                    for i in range(2):
                        vk = 2 * c + i - 16 * stp
                        off = 1920 - 128 * vk
                        sl = psc[:, 1024 * (c % 2) + 512 * i:
                                 1024 * (c % 2) + 512 * (i + 1)]
                        mm = v.tensor_tensor(sl, sl, strip_sb[:, off:off + 512],
                                             ALU.add)
                    return mm

                def ex_op(s, c=0):
                    return s.activation(
                        P[c % 2][:, :],
                        psc[:, 1024 * (c % 2):1024 * (c % 2) + 1024],
                        AF.Exp, bias=self.cbias[:, 0:1])

                def av_op(t, h=h, c=0, ncs=ncs):
                    first, last = (c == 0), (c == ncs - 1)
                    for i in range(2):
                        k = 2 * c + i
                        t.matmul(pav[:, :], vb[h % 2][:, 128 * k:128 * (k + 1)],
                                 P[c % 2][:, 512 * i:512 * (i + 1)],
                                 start=(first and i == 0),
                                 stop=(last and i == 1))
                    for i in range(2):
                        mm = t.matmul(pl[0:1, 0:512], self.onec_b[:, :],
                                      P[c % 2][:, 512 * i:512 * (i + 1)],
                                      start=(first and i == 0),
                                      stop=(last and i == 1))
                    return mm

                for c in range(ncs):
                    pb = f"bpsc{c % 2}"
                    S.add("tensor",
                          lambda t, h=h, stp=stp, c=c: sc_op(t, h, stp, c),
                          reads=[f"bkT{h % 2}", f"bq{h % 2}"], writes=[pb])
                    if c >= 8 * stp:
                        S.add("vector",
                              lambda v, stp=stp, c=c: mk_op(v, stp, c),
                              reads=[pb, "strip"], writes=[pb])
                    S.add("scalar", lambda s, c=c: ex_op(s, c),
                          reads=[pb], writes=[f"bP{c % 2}", pb])
                    S.add("tensor",
                          lambda t, h=h, c=c, ncs=ncs: av_op(t, h, c, ncs),
                          reads=[f"bP{c % 2}", f"bv{h % 2}", "consts"],
                          writes=["bpav", "bpl", f"bP{c % 2}"])

                def ep_act(s):
                    s.activation(lnl[:, :], pl[0:1, 0:512], AF.Ln)
                    s.activation(rrow[:, :], lnl[:, :], AF.Exp, scale=-1.0)
                    return s.activation(rhi[:, :], rrow[:, :], AF.Copy)
                S.add("scalar", ep_act, reads=["bpl"], writes=["brr", "bpl"])

                def ep_lo(v):
                    v.tensor_sub(rlo_f[:, :], rrow[:, :], rhi[:, :])
                    v.tensor_copy(self.gapb[0:1, 0:4], self.gapb[0:1, 0:4])
                    v.tensor_copy(self.gapb[:, 2:3], self.gapb[:, 0:1])
                    return v.tensor_copy(rlo[:, :], rlo_f[:, :])
                S.add("vector", ep_lo, reads=["brr"], writes=["brlo"])

                def ep_bc(t):
                    t.matmul(prb[:, :], self.ones_b[:, 0:128], rhi[:, :],
                             start=True, stop=False)
                    return t.matmul(prb[:, :], self.ones_b[:, 0:128], rlo[:, :],
                                    start=False, stop=True)
                S.add("tensor", ep_bc,
                      reads=["brr", "brlo", "consts"], writes=["bprb"])

                def ep_ev(v, h=h, stp=stp):
                    v.tensor_copy(rb[:, :], prb[:, :])
                    return v.tensor_tensor(
                        attnT[:, 2048 * h + 512 * stp:
                              2048 * h + 512 * (stp + 1)],
                        pav[:, :], rb[:, :], ALU.mult)
                S.add("vector", ep_ev,
                      reads=["bprb", "bpav"],
                      writes=["attnT", "bpav", "bprb"])
                pump()
                if gs % 2 == 0:
                    pump()
        while job_k[0] < len(jobs):
            pump()
        S.emit("b")

    def d_c(self, st, attnT):
        S = Sched(self.nc, self.sems)
        wo_sb = self.sb(st, "C_wo", [128, NDC * D], BF16)
        bob = self.sb(st, "C_bob", [1, D], BF16)
        stg0 = self.sb(st, "C_stg0", [128, D], F32)
        stg = [stg0, stg0]
        gwb = self.sb(st, "C_gwb", [128, NDC * E], BF16)
        xo = [self.sb(st, f"C_xo{i}", [128, D], F32) for i in range(2)]
        x1lo = self.sb(st, "C_x1lo", [128, D], BF16)
        x1Tl = self.sb(st, "C_x1Tl", [128, D], BF16)
        gwlo = self.sb(st, "C_gwlo", [128, NDC * E], BF16)
        x1f = [self.sb(st, f"C_x1f{i}", [128, D], F32) for i in range(2)]
        x1b = self.sb(st, "C_x1b", [128, D], BF16)
        x1T = [self.sb(st, f"C_x1T{i}", [128, D], BF16) for i in range(2)]
        s1 = self.sb(st, "C_s1", [128, 1], F32)
        mean = self.sb(st, "C_mean", [128, 1], F32)
        m2 = self.sb(st, "C_m2", [128, 1], F32)
        ssq = self.sb(st, "C_ssq", [128, 1], F32)
        var = self.sb(st, "C_var", [128, 1], F32)
        sd = self.sb(st, "C_sd", [128, 1], F32)
        rstd = self.sb(st, "C_rstd", [128, 1], F32)
        gt = self.sb(st, "C_gt", [128, E], F32)
        gd = self.sb(st, "C_gd", [128, E], F32)
        geq1 = self.sb(st, "C_geq1", [128, E], F32)
        geq2 = self.sb(st, "C_geq2", [128, E], F32)
        gm1 = self.sb(st, "C_gm1", [128, 1], F32)
        gm2 = self.sb(st, "C_gm2", [128, 1], F32)
        gdm = self.sb(st, "C_gdm", [128, 1], F32)
        gg1 = self.sb(st, "C_gg1", [128, 1], F32)
        gg2 = self.sb(st, "C_gg2", [128, 1], F32)
        psw = self.psum(st, "C_psw", [128, 2048], F32)
        psg = self.psum(st, "C_psg", [128, 512], F32)
        pst = self.psum(st, "C_pst", [128, 2048], BF16)
        pwt = self.psum(st, "C_pwt", [128, 128], F32)

        self._castload(S, wo_sb, self.wo, NDC, "C", stg)
        self._bias(S, bob, self.bo, "C", stg)
        S.add("sync",
              lambda sy: sy.dma_start(
                  _ap(stg[0], 0, [[D, 128], [E, NDC], [1, E]]),
                  _ap(self.gwt, 0, [[E, 128], [128 * E, NDC], [1, E]])),
              writes=["Cstg0"], dma=True)
        def gwcast(v):
            v.tensor_copy(gwb[:, :], stg[0][:, 0:NDC * E])
            v.tensor_copy(self.gapb[:, 2:3], self.gapb[:, 0:1])
            v.tensor_copy(self.gapb[:, 3:4], self.gapb[:, 0:1])
            return v.tensor_sub(gwlo[:, :], stg[0][:, 0:NDC * E], gwb[:, :])
        S.add("vector", gwcast, reads=["Cstg0"], writes=["Cgw"])
        wreads = [f"Cw{i}" for i in range(NDC)] + ["Cb"]
        for ti in range(16):
            S.add("sync",
                  lambda sy, ti=ti: sy.dma_start(
                      xo[ti % 2][:, :], self.x_own[128 * ti:128 * (ti + 1), :]),
                  writes=[f"Cxo{ti % 2}", "Chres"], dma=True)

            def wo_op(t, ti=ti):
                for oc in range(4):
                    bank = psw[:, 512 * oc:512 * (oc + 1)]
                    for hd in range(NDC):
                        t.matmul(bank,
                                 attnT[:, 2048 * hd + 128 * ti:
                                       2048 * hd + 128 * (ti + 1)],
                                 wo_sb[:, D * hd + 512 * oc:
                                       D * hd + 512 * (oc + 1)],
                                 start=(hd == 0), stop=False)
                    mm = t.matmul(bank, self.ones_b[:, 0:128],
                                  bob[:, 512 * oc:512 * (oc + 1)],
                                  start=False, stop=True)
                return mm
            S.add("tensor", wo_op, reads=["attnT", "consts"] + wreads,
                  writes=["Cpsw"])
            S.add("vector",
                  lambda v, ti=ti: v.tensor_add(xo[ti % 2][:, :], psw[:, :],
                                                xo[ti % 2][:, :]),
                  reads=["Cpsw", f"Cxo{ti % 2}"],
                  writes=["Chres", f"Cxo{ti % 2}"])

            if self.debug:
                S.add("sync",
                      lambda sy, ti=ti: sy.dma_start(
                          self.hres_d[128 * ti:128 * (ti + 1), :], xo[ti % 2][:, :]),
                      reads=["Chres"], writes=["dbg"], dma=True)

            def stats(v, ti=ti):
                v.reduce_sum(s1[:, :], xo[ti % 2][:, :], axis=AX.X)
                v.tensor_copy(self.gapb[:, 2:3], self.gapb[:, 0:1])
                v.tensor_copy(self.gapb[:, 3:4], self.gapb[:, 0:1])
                return v.tensor_scalar_mul(mean[:, :], s1[:, :], 1.0 / D)
            S.add("vector", stats, reads=["Chres"], writes=["Cmean"])
            def sqsum(v, ti=ti):
                v.tensor_tensor(psw[:, :], xo[ti % 2][:, :], xo[ti % 2][:, :], ALU.mult)
                return v.reduce_sum(ssq[:, :], psw[:, :], axis=AX.X)
            S.add("vector", sqsum, reads=["Chres"], writes=["Cssq", "Cpsw"])

            def varop(v):
                v.tensor_tensor(m2[:, :], mean[:, :], mean[:, :], ALU.mult)
                v.tensor_copy(self.gapb[:, 2:3], self.gapb[:, 0:1])
                v.tensor_copy(self.gapb[:, 3:4], self.gapb[:, 0:1])
                return v.scalar_tensor_tensor(var[:, :], ssq[:, :], 1.0 / D,
                                              m2[:, :], ALU.mult, ALU.subtract)
            S.add("vector", varop, reads=["Cmean", "Cssq"], writes=["Cvar"])
            S.add("scalar",
                  lambda s: s.activation(sd[:, :], var[:, :], AF.Sqrt,
                                         bias=self.cbias[:, 1:2]),
                  reads=["Cvar"], writes=["Csd"])
            if self.debug:
                def dmp(v, ti=ti):
                    v.tensor_copy(gt[:, 0:1], mean[:, :])
                    v.tensor_copy(gt[:, 1:2], ssq[:, :])
                    v.tensor_copy(gt[:, 2:3], var[:, :])
                    return v.tensor_copy(gt[:, 3:4], sd[:, :])
                S.add("vector", dmp, reads=["Cmean", "Cssq", "Cvar", "Csd"],
                      writes=["Cgt"])
                S.add("sync",
                      lambda sy, ti=ti: sy.dma_start(
                          self.st_d[128 * ti:128 * (ti + 1), :], gt[:, :]),
                      reads=["Cgt"], writes=["dbg"], dma=True)

            def norm(v, ti=ti):
                v.reciprocal(rstd[:, :], sd[:, :])
                v.tensor_copy(self.gapb[:, 2:3], self.gapb[:, 0:1])
                v.tensor_copy(self.gapb[:, 3:4], self.gapb[:, 0:1])
                return v.tensor_scalar(x1f[ti % 2][:, :], xo[ti % 2][:, :],
                                       mean[:, :], rstd[:, :],
                                       ALU.subtract, ALU.mult)
            S.add("vector", norm, reads=["Csd", "Chres", "Cmean"],
                  writes=[f"Cx1f{ti % 2}", "Chres"])
            S.add("scalar",
                  lambda s, ti=ti: s.activation(x1b[:, :], x1f[ti % 2][:, :],
                                                AF.Copy),
                  reads=[f"Cx1f{ti % 2}"], writes=["Cx1b"])
            S.add("vector",
                  lambda v, ti=ti: v.tensor_sub(x1lo[:, :], x1f[ti % 2][:, :],
                                                x1b[:, :]),
                  reads=[f"Cx1f{ti % 2}", "Cx1b"], writes=["Cx1lo"])
            S.add("sync",
                  lambda sy, ti=ti: sy.dma_start(
                      self.x1_s[128 * ti:128 * (ti + 1), :],
                      x1f[ti % 2][:, :]),
                  reads=[f"Cx1f{ti % 2}"], writes=["x1_s"], dma=True)
            if C_MIN:
                continue
            for b4 in range(4):
                ph = f"Cpst{b4 % 2}"

                def tr(t, b4=b4):
                    return _tr4(t, pst, 1024 * (b4 % 2),
                                [x1b[:, 128 * (4 * b4 + k):
                                      128 * (4 * b4 + k + 1)]
                                 for k in range(4)], self.ident_b[:, :])
                S.add("tensor", tr, reads=["Cx1b", "consts"], writes=[ph])
                S.add("vector",
                      lambda v, ti=ti, b4=b4: v.tensor_copy(
                          x1T[ti % 2][:, 512 * b4:512 * (b4 + 1)],
                          pst[:, 1024 * (b4 % 2):1024 * (b4 % 2) + 512]),
                      reads=[ph], writes=[f"Cx1T{ti % 2}b{b4}"])
            treads = [f"Cx1T{ti % 2}b{b}" for b in range(4)]
            S.add("sync",
                  lambda sy, ti=ti: sy.dma_start(
                      _ap(self.x1T_s, 128 * ti,
                          [[NDC * 2048, 128], [2048, NDC], [1, 128]]),
                      x1T[ti % 2][:, :]),
                  reads=treads, writes=["x1T_s"], dma=True)

            for b4 in range(4):
                ph = f"Cpstl{b4 % 2}"
                S.add("tensor",
                      lambda t, b4=b4: _tr4(
                          t, pst, 1024 * (b4 % 2),
                          [x1lo[:, 128 * (4 * b4 + k):128 * (4 * b4 + k + 1)]
                           for k in range(4)], self.ident_b[:, :]),
                      reads=["Cx1lo", "consts"], writes=[ph])
                S.add("vector",
                      lambda v, b4=b4: v.tensor_copy(
                          x1Tl[:, 512 * b4:512 * (b4 + 1)],
                          pst[:, 1024 * (b4 % 2):1024 * (b4 % 2) + 512]),
                      reads=[ph], writes=[f"Cx1Tlb{b4}"])
            tlreads = [f"Cx1Tlb{b}" for b in range(4)]

            def gate_mm(t, ti=ti):
                terms = [(x1T[ti % 2], gwb), (x1T[ti % 2], gwlo), (x1Tl, gwb)]
                n = len(terms) * NDC
                k = 0
                for xt, gw in terms:
                    for dc in range(NDC):
                        mm = t.matmul(psg[:, 0:E],
                                      xt[:, 128 * dc:128 * (dc + 1)],
                                      gw[:, E * dc:E * (dc + 1)],
                                      start=(k == 0), stop=(k == n - 1))
                        k += 1
                return mm
            S.add("tensor", gate_mm, reads=treads + tlreads + ["Cgw"],
                  writes=["Cpsg"])

            def gate1(v):
                g = lambda: v.tensor_copy(self.gapb[:, 2:3], self.gapb[:, 0:1])
                v.tensor_copy(gt[:, :], psg[:, 0:E])
                g(); g()
                v.reduce_max(gm1[:, :], gt[:, :], axis=AX.X)
                g(); g()
                v.tensor_scalar(gd[:, :], gt[:, :], gm1[:, :], None,
                                ALU.subtract)
                g(); g()
                v.tensor_single_scalar(geq1[:, :], gd[:, :], 0.0, ALU.is_equal)
                g(); g()
                v.scalar_tensor_tensor(gd[:, :], geq1[:, :], NEG, gt[:, :],
                                       ALU.mult, ALU.add)
                g(); g()
                v.reduce_max(gm2[:, :], gd[:, :], axis=AX.X)
                g(); g()
                v.tensor_scalar(gd[:, :], gd[:, :], gm2[:, :], None,
                                ALU.subtract)
                g(); g()
                v.tensor_single_scalar(geq2[:, :], gd[:, :], 0.0, ALU.is_equal)
                g(); g()
                return v.tensor_sub(gdm[:, :], gm1[:, :], gm2[:, :])
            S.add("vector", gate1, reads=["Cpsg"], writes=["Cg1", "Cpsg"])

            def gate_sig(s):
                s.activation(gg1[:, :], gdm[:, :], AF.Sigmoid)
                return s.activation(gg2[:, :], gdm[:, :], AF.Sigmoid,
                                    scale=-1.0)
            S.add("scalar", gate_sig, reads=["Cg1"], writes=["Cgg"])

            def gate2(v, ti=ti):
                v.tensor_scalar(geq1[:, :], geq1[:, :], gg1[:, :], None,
                                ALU.mult)
                v.tensor_copy(self.gapb[:, 2:3], self.gapb[:, 0:1])
                v.tensor_copy(self.gapb[:, 3:4], self.gapb[:, 0:1])
                return v.scalar_tensor_tensor(
                    self.wf_sb[:, E * ti:E * (ti + 1)], geq2[:, :], gg2[:, :],
                    geq1[:, :], ALU.mult, ALU.add)
            S.add("vector", gate2, reads=["Cgg", "Cg1"], writes=["wf"])
            S.add("tensor",
                  lambda t, ti=ti: t.transpose(
                      pwt[0:E, 0:128],
                      _ap(self.wf_sb, E * ti, [[16 * E, 128], [1, E]]),
                      self.ident_f[:, :]),
                  reads=["wf", "consts"], writes=["Cpwt"])
            S.add("vector",
                  lambda v, ti=ti: v.tensor_copy(
                      self.wT_sb[0:E, 128 * ti:128 * (ti + 1)], pwt[0:E, :]),
                  reads=["Cpwt"], writes=["wT"])
        S.emit("c")

    def d_f(self, st):
        S = Sched(self.nc, self.sems)
        x1T = self.sb(st, "F_x1T", [128, NDC * 512], BF16)
        a1b = self.sb(st, "F_a1b", [128, NDC * E * 16], BF16)
        a3b = self.sb(st, "F_a3b", [128, NDC * E * 16], BF16)
        a2b = self.sb(st, "F_a2b", [128, E * NFC * 16], BF16)
        b1e = self.sb(st, "F_b1e", [16, FF], BF16)
        b3e = self.sb(st, "F_b3e", [16, FF], BF16)
        b2e = self.sb(st, "F_b2e", [16, D], BF16)
        wstr = [self.sb(st, f"F_wstr{i}", [128, 2048], BF16) for i in range(2)]
        bstg = [self.sb(st, f"F_bstg{i}", [128, 512], BF16) for i in range(2)]
        bstr = [self.sb(st, f"F_bstr{i}", [128, 1024], BF16) for i in range(2)]
        u1s = self.sb(st, "F_u1s", [16, 512], BF16)
        u3s = self.sb(st, "F_u3s", [16, 512], BF16)
        u2s = self.sb(st, "F_u2s", [16, 512], BF16)
        h1b = [self.sb(st, f"F_h1b{i}", [128, 1024], BF16) for i in range(2)]
        sil = [self.sb(st, f"F_sil{i}", [128, 1024], BF16) for i in range(2)]
        h3b = [self.sb(st, f"F_h3b{i}", [128, 1024], BF16) for i in range(2)]
        a_sb = self.sb(st, "F_a", [128, NFC * 512], BF16)
        acc = self.sb(st, "F_acc", [128, NDC * 512], F32)
        wrow = self.sb(st, "F_wrow", [1, 512], F32)
        whi = self.sb(st, "F_whi", [1, 512], BF16)
        wlo_f = self.sb(st, "F_wlo_f", [1, 512], F32)
        wlo = self.sb(st, "F_wlo", [1, 512], BF16)
        wbc = self.sb(st, "F_wbc", [128, 512], F32)
        tmp = self.sb(st, "F_tmp", [128, 1024], F32)
        x1t = self.sb(st, "F_x1t", [128, D], F32)
        hres = self.sb(st, "F_hres", [128, D], F32)
        yt = hres
        s1 = self.sb(st, "F_s1", [128, 1], F32)
        mean = self.sb(st, "F_mean", [128, 1], F32)
        m2 = self.sb(st, "F_m2", [128, 1], F32)
        ssq = self.sb(st, "F_ssq", [128, 1], F32)
        var = self.sb(st, "F_var", [128, 1], F32)
        sd = self.sb(st, "F_sd", [128, 1], F32)
        rstd = self.sb(st, "F_rstd", [128, 1], F32)
        psA = self.psum(st, "F_psA", [128, 4096], F32)
        PSB = [f"psb{i}" for i in range(8)]

        def bank(i, rows=128):
            return psA[0:rows, 512 * i:512 * (i + 1)]

        for src, dst, nm in [(self.a1c, a1b, "Fa1"), (self.a3c, a3b, "Fa3"),
                             (self.a2c, a2b, "Fa2")]:
            S.add("sync", lambda sy, src=src, dst=dst: sy.dma_start(
                dst[:, :], src[:, :]), reads=["smallc"], writes=[nm], dma=True)
        smreads = ["Fa1", "Fa3", "Fa2"]

        for stp in range(4):
            S.add("sync",
                  lambda sy, stp=stp: sy.dma_start(
                      x1T[:, :],
                      _ap(self.x1T_s, 512 * stp,
                          [[NDC * 2048, 128], [2048, NDC], [1, 512]])),
                  reads=["x1T_s"], writes=["Fx1T"], dma=True)
            for m in range(2):
                wsrc = self.w1c if m == 0 else self.w3c
                bdst = self.b1_s if m == 0 else self.b3_s
                bres = "Fb1_s" if m == 0 else "Fb3_s"
                for fc in range(NFC):
                    k = m * NFC + fc
                    S.add("sync",
                          lambda sy, wsrc=wsrc, fc=fc, k=k: sy.dma_start(
                              wstr[k % 2][:, :],
                              _ap(wsrc, 128 * fc,
                                  [[FF, 128], [128 * FF, NDC], [1, 128]])),
                          reads=["wXc"], writes=[f"Fws{k % 2}"], dma=True)

                    def base_mm(t, fc=fc, k=k):
                        bk = bank(k % 2)
                        for dc in range(NDC):
                            mm = t.matmul(bk,
                                          wstr[k % 2][:, 128 * dc:128 * (dc + 1)],
                                          x1T[:, 512 * dc:512 * (dc + 1)],
                                          start=(dc == 0), stop=(dc == NDC - 1))
                        return mm
                    S.add("tensor", base_mm, reads=[f"Fws{k % 2}", "Fx1T"],
                          writes=[PSB[k % 2], f"Fws{k % 2}"])
                    S.add("scalar",
                          lambda s, k=k: s.activation(bstg[k % 2][:, :],
                                                      bank(k % 2), AF.Copy),
                          reads=[PSB[k % 2]],
                          writes=[f"Fbe{k % 2}", PSB[k % 2]])
                    S.add("sync",
                          lambda sy, bdst=bdst, fc=fc, k=k: sy.dma_start(
                              bdst[128 * fc:128 * (fc + 1), :],
                              bstg[k % 2][:, :]),
                          reads=[f"Fbe{k % 2}"], writes=[bres], dma=True)
            for e in range(E):
                S.add("sync",
                      lambda sy, e=e: sy.dma_start(
                          b1e[:, :], self.b1c[:, FF * e:FF * (e + 1)]),
                      reads=["smallc"], writes=["Fb1e"], dma=True)
                S.add("sync",
                      lambda sy, e=e: sy.dma_start(
                          b3e[:, :], self.b3c[:, FF * e:FF * (e + 1)]),
                      reads=["smallc"], writes=["Fb3e"], dma=True)
                S.add("sync",
                      lambda sy, e=e: sy.dma_start(
                          b2e[:, :], self.b2c[:, D * e:D * (e + 1)]),
                      reads=["smallc"], writes=["Fb2e"], dma=True)
                for uu, (ab, us, bk_i, nm) in enumerate(
                        [(a1b, u1s, 4, "Fu1"), (a3b, u3s, 5, "Fu3")]):
                    def u_mm(t, e=e, ab=ab, bk_i=bk_i):
                        bk = bank(bk_i, 16)
                        for dc in range(NDC):
                            mm = t.matmul(bk,
                                          ab[:, (e * NDC + dc) * 16:
                                             (e * NDC + dc) * 16 + 16],
                                          x1T[:, 512 * dc:512 * (dc + 1)],
                                          start=(dc == 0), stop=(dc == NDC - 1))
                        return mm
                    S.add("tensor", u_mm, reads=["Fx1T"] + smreads,
                          writes=[PSB[bk_i]])
                    S.add("scalar",
                          lambda s, us=us, bk_i=bk_i: s.activation(
                              us[:, :], bank(bk_i, 16), AF.Copy, scale=2.0),
                          reads=[PSB[bk_i]], writes=[nm, PSB[bk_i]])
                for fp in range(NFC // 2):
                    par = fp % 2
                    S.add("sync",
                          lambda sy, fp=fp, par=par: sy.dma_start(
                              bstr[par][:, :],
                              _ap(self.b1_s, 256 * fp * 512,
                                  [[512, 128], [128 * 512, 2], [1, 512]])),
                          reads=["Fb1_s"], writes=[f"Fbr{par}"], dma=True)

                    def h1_mm(t, e=e, fp=fp, par=par):
                        for hh in range(2):
                            fc = 2 * fp + hh
                            mm = t.matmul(
                                psA[:, 1024 * par + 512 * hh:
                                    1024 * par + 512 * (hh + 1)],
                                b1e[:, 128 * fc:128 * (fc + 1)],
                                u1s[:, :], start=True, stop=True)
                        return mm
                    S.add("tensor", h1_mm, reads=["Fu1", "Fb1e"],
                          writes=[PSB[2 * par], PSB[2 * par + 1]])
                    S.add("vector",
                          lambda v, par=par: v.tensor_add(
                              h1b[par][:, :], psA[:, 1024 * par:1024 * (par + 1)],
                              bstr[par][:, :]),
                          reads=[PSB[2 * par], PSB[2 * par + 1], f"Fbr{par}"],
                          writes=[f"Fh1{par}", PSB[2 * par], PSB[2 * par + 1]])
                    S.add("scalar",
                          lambda s, par=par: s.activation(
                              sil[par][:, :], h1b[par][:, :], AF.Silu),
                          reads=[f"Fh1{par}"], writes=[f"Fsil{par}"])
                    S.add("sync",
                          lambda sy, fp=fp, par=par: sy.dma_start(
                              bstr[par][:, :],
                              _ap(self.b3_s, 256 * fp * 512,
                                  [[512, 128], [128 * 512, 2], [1, 512]])),
                          reads=["Fb3_s"], writes=[f"Fbr{par}"], dma=True)

                    def h3_mm(t, e=e, fp=fp, par=par):
                        for hh in range(2):
                            fc = 2 * fp + hh
                            mm = t.matmul(
                                psA[:, 2048 + 1024 * par + 512 * hh:
                                    2048 + 1024 * par + 512 * (hh + 1)],
                                b3e[:, 128 * fc:128 * (fc + 1)],
                                u3s[:, :], start=True, stop=True)
                        return mm
                    S.add("tensor", h3_mm, reads=["Fu3", "Fb3e"],
                          writes=[PSB[4 + 2 * par], PSB[5 + 2 * par]])
                    S.add("vector",
                          lambda v, par=par: v.tensor_add(
                              h3b[par][:, :],
                              psA[:, 2048 + 1024 * par:2048 + 1024 * (par + 1)],
                              bstr[par][:, :]),
                          reads=[PSB[4 + 2 * par], PSB[5 + 2 * par],
                                 f"Fbr{par}"],
                          writes=[f"Fh3{par}", PSB[4 + 2 * par],
                                  PSB[5 + 2 * par]])
                    S.add("gpsimd",
                          lambda g, fp=fp, par=par: g.tensor_mul(
                              a_sb[:, 1024 * fp:1024 * (fp + 1)],
                              sil[par][:, :], h3b[par][:, :]),
                          reads=[f"Fsil{par}", f"Fh3{par}"],
                          writes=[f"Fa{fp}", f"Fsil{par}", f"Fh3{par}"])
                areads = [f"Fa{fp}" for fp in range(NFC // 2)]
                S.add("sync",
                      lambda sy, stp=stp, e=e: sy.dma_start(
                          wrow[:, :],
                          _ap(self.wT_sb, 2048 * e + 512 * stp,
                              [[2048, 1], [1, 512]])),
                      reads=["wT"], writes=["Fwrow"], dma=True)

                def wsplit(v):
                    v.tensor_copy(whi[:, :], wrow[:, :])
                    v.tensor_copy(self.gapb[:, 2:3], self.gapb[:, 0:1])
                    v.tensor_copy(self.gapb[:, 3:4], self.gapb[:, 0:1])
                    v.tensor_sub(wlo_f[:, :], wrow[:, :], whi[:, :])
                    v.tensor_copy(self.gapb[:, 2:3], self.gapb[:, 0:1])
                    v.tensor_copy(self.gapb[:, 3:4], self.gapb[:, 0:1])
                    return v.tensor_copy(wlo[:, :], wlo_f[:, :])
                S.add("vector", wsplit, reads=["Fwrow"], writes=["Fwhl"])

                def wbc_mm(t):
                    t.matmul(bank(7), self.ones_b[:, 0:128], whi[:, :],
                             start=True, stop=False)
                    return t.matmul(bank(7), self.ones_b[:, 0:128], wlo[:, :],
                                    start=False, stop=True)
                S.add("tensor", wbc_mm, reads=["Fwhl", "consts"],
                      writes=[PSB[7]])
                S.add("vector",
                      lambda v: v.tensor_copy(wbc[:, :], bank(7)),
                      reads=[PSB[7]], writes=["Fwbc", PSB[7]])
                def u2_mm(t, e=e):
                    bk = bank(6, 16)
                    for fc in range(NFC):
                        mm = t.matmul(bk,
                                      a2b[:, (e * NFC + fc) * 16:
                                          (e * NFC + fc) * 16 + 16],
                                      a_sb[:, 512 * fc:512 * (fc + 1)],
                                      start=(fc == 0), stop=(fc == NFC - 1))
                    return mm
                S.add("tensor", u2_mm, reads=areads + smreads,
                      writes=[PSB[6]])
                S.add("scalar",
                      lambda s: s.activation(u2s[:, :], bank(6, 16),
                                             AF.Copy, scale=2.0),
                      reads=[PSB[6]], writes=["Fu2", PSB[6]])
                for half in range(2):
                    for fc in range(NFC):
                        k2 = half * NFC + fc
                        S.add("sync",
                              lambda sy, half=half, fc=fc, k2=k2: sy.dma_start(
                                  wstr[k2 % 2][:, 0:1024],
                                  self.w2c[128 * fc:128 * (fc + 1),
                                           1024 * half:1024 * (half + 1)]),
                              reads=["wXc"], writes=[f"Fws{k2 % 2}"], dma=True)

                        def w2_mm(t, half=half, fc=fc, k2=k2):
                            for b in range(8):
                                mm = t.matmul(
                                    psA[:, 512 * b:512 * (b + 1)],
                                    wstr[k2 % 2][:, 128 * b:128 * (b + 1)],
                                    a_sb[:, 512 * fc:512 * (fc + 1)],
                                    start=(fc == 0), stop=False)
                            return mm
                        S.add("tensor", w2_mm,
                              reads=[f"Fws{k2 % 2}"] + areads,
                              writes=PSB + [f"Fws{k2 % 2}"])

                    def b2_mm(t, e=e, half=half):
                        for b in range(8):
                            dc = 8 * half + b
                            mm = t.matmul(
                                psA[:, 512 * b:512 * (b + 1)],
                                b2e[:, 128 * dc:128 * (dc + 1)],
                                u2s[:, :], start=False, stop=True)
                        return mm
                    S.add("tensor", b2_mm, reads=["Fu2", "Fb2e"], writes=PSB)
                    for q in range(4):
                        def ev_mult(v, half=half, q=q):
                            for dk in range(2):
                                b = 2 * q + dk
                                mm = v.tensor_tensor(
                                    _ap(tmp, 512 * dk, [[1024, 128], [1, 512]]),
                                    psA[:, 512 * b:512 * (b + 1)],
                                    wbc[:, 0:512], ALU.mult)
                            return mm
                        S.add("vector", ev_mult,
                              reads=[PSB[2 * q], PSB[2 * q + 1], "Fwbc"],
                              writes=["Ftmp"])

                        def ev_add(g, e=e, half=half, q=q):
                            for dk in range(2):
                                dc = 8 * half + 2 * q + dk
                                dst = acc[:, 512 * dc:512 * (dc + 1)]
                                src = _ap(tmp, 512 * dk, [[1024, 128], [1, 512]])
                                if e == 0:
                                    mm = g.tensor_copy(dst, src)
                                else:
                                    mm = g.tensor_add(dst, dst, src)
                            return mm
                        S.add("gpsimd", ev_add, reads=["Ftmp"],
                              writes=[f"Facc{half}q{q}", "Ftmp",
                                      PSB[2 * q], PSB[2 * q + 1]])
            accreads = [f"Facc{h}q{q}" for h in range(2) for q in range(4)]
            if self.debug:
                S.add("sync",
                      lambda sy, stp=stp: sy.dma_start(
                          self.acc_d[128 * stp:128 * (stp + 1), :], acc[:, :]),
                      reads=accreads, writes=["dbg"], dma=True)
                if stp == 3:
                    S.add("sync",
                          lambda sy: sy.dma_start(self.a_d[:, :], a_sb[:, :]),
                          reads=[f"Fa{fp}" for fp in range(NFC // 2)],
                          writes=["dbg"], dma=True)

                    def udmp(v):
                        v.tensor_copy(self.junkrow[0:16, 0:512], u1s[:, :])
                        v.tensor_copy(self.junkrow[0:16, 512:1024], u3s[:, :])
                        return v.tensor_copy(self.junkrow[0:16, 1024:1536],
                                             u2s[:, :])
                    S.add("vector", udmp, reads=["Fu1", "Fu3", "Fu2"],
                          writes=["dbgu"])
                    S.add("sync",
                          lambda sy: sy.dma_start(
                              _ap(self.u_d, 0, [[1536, 16], [1, 1536]]),
                              self.junkrow[0:16, 0:1536]),
                          reads=["dbgu"], writes=["dbg"], dma=True)
            for sub in range(4):
                ti = 4 * stp + sub
                S.add("sync",
                      lambda sy, ti=ti: sy.dma_start(
                          x1t[:, :], self.x1_s[128 * ti:128 * (ti + 1), :]),
                      reads=["x1_s"], writes=["Fx1t"], dma=True)
                for b4 in range(4):
                    ph = f"Fpt{b4 % 2}"
                    pbs = [PSB[4 + 2 * (b4 % 2)], PSB[5 + 2 * (b4 % 2)]]

                    def trF(t, sub=sub, b4=b4):
                        base = 2048 + 1024 * (b4 % 2)
                        return _tr4(t, psA, base,
                                    [acc[:, 512 * (4 * b4 + k) + 128 * sub:
                                         512 * (4 * b4 + k) + 128 * (sub + 1)]
                                     for k in range(4)], self.ident_f[:, :])
                    S.add("tensor", trF, reads=accreads + ["consts"],
                          writes=[ph] + pbs)
                    S.add("vector",
                          lambda v, sub=sub, b4=b4: v.tensor_add(
                              hres[:, 512 * b4:512 * (b4 + 1)],
                              _ap(psA, 2048 + 1024 * (b4 % 2),
                                  [[4096, 128], [1, 512]]),
                              x1t[:, 512 * b4:512 * (b4 + 1)]),
                          reads=[ph, "Fx1t"], writes=[f"Fhr{b4}"] + pbs)
                hreads = [f"Fhr{b}" for b in range(4)]

                def stats(v):
                    v.reduce_sum(s1[:, :], hres[:, :], axis=AX.X)
                    v.tensor_copy(self.gapb[:, 2:3], self.gapb[:, 0:1])
                    v.tensor_copy(self.gapb[:, 3:4], self.gapb[:, 0:1])
                    return v.tensor_scalar_mul(mean[:, :], s1[:, :], 1.0 / D)
                S.add("vector", stats, reads=hreads, writes=["Fmean"])
                def sqsum(v):
                    v.tensor_tensor(psA[:, 0:2048], hres[:, :], hres[:, :],
                                    ALU.mult)
                    return v.reduce_sum(ssq[:, :], psA[:, 0:2048], axis=AX.X)
                S.add("vector", sqsum, reads=hreads,
                      writes=["Fssq"] + PSB[0:4])

                def varop(v):
                    v.tensor_tensor(m2[:, :], mean[:, :], mean[:, :], ALU.mult)
                    v.tensor_copy(self.gapb[:, 2:3], self.gapb[:, 0:1])
                    v.tensor_copy(self.gapb[:, 3:4], self.gapb[:, 0:1])
                    return v.scalar_tensor_tensor(var[:, :], ssq[:, :],
                                                  1.0 / D, m2[:, :],
                                                  ALU.mult, ALU.subtract)
                S.add("vector", varop, reads=["Fmean", "Fssq"],
                      writes=["Fvar"])
                S.add("scalar",
                      lambda s: s.activation(sd[:, :], var[:, :], AF.Sqrt,
                                             bias=self.cbias[:, 1:2]),
                      reads=["Fvar"], writes=["Fsd"])

                def norm(v):
                    v.reciprocal(rstd[:, :], sd[:, :])
                    v.tensor_copy(self.gapb[:, 2:3], self.gapb[:, 0:1])
                    v.tensor_copy(self.gapb[:, 3:4], self.gapb[:, 0:1])
                    return v.tensor_scalar(yt[:, :], hres[:, :], mean[:, :],
                                           rstd[:, :], ALU.subtract, ALU.mult)
                S.add("vector", norm, reads=["Fsd", "Fmean"] + hreads,
                      writes=["Fyt"] + hreads)
                S.add("sync",
                      lambda sy, ti=ti: sy.dma_start(
                          self.y[128 * ti:128 * (ti + 1), :], yt[:, :]),
                      reads=["Fyt"], writes=["ydone"], dma=True)
        S.emit("f")


def _host_inputs(inputs):
    x = np.ascontiguousarray(np.asarray(inputs["L0_attn_x"], np.float32))
    base = {
        "wq": np.ascontiguousarray(np.asarray(inputs["L0_attn_Wq"], np.float32)),
        "wk": np.ascontiguousarray(np.asarray(inputs["L0_attn_Wk"], np.float32)),
        "wv": np.ascontiguousarray(np.asarray(inputs["L0_attn_Wv"], np.float32)),
        "wo": np.ascontiguousarray(np.asarray(inputs["L0_attn_Wo"], np.float32)),
        "bq": np.asarray(inputs["L0_attn_bq"], np.float32).reshape(1, D),
        "bk": np.asarray(inputs["L0_attn_bk"], np.float32).reshape(1, D),
        "bv": np.asarray(inputs["L0_attn_bv"], np.float32).reshape(1, D),
        "bo": np.asarray(inputs["L0_attn_bo"], np.float32).reshape(1, D),
        "gwt": np.ascontiguousarray(np.asarray(inputs["L0_ffn_gate_w"],
                                               np.float32).T),
        "w1t": np.ascontiguousarray(np.asarray(inputs["L0_ffn_w1"],
                                               np.float32).T),
        "w3t": np.ascontiguousarray(np.asarray(inputs["L0_ffn_w3"],
                                               np.float32).T),
        "w2t": np.ascontiguousarray(np.asarray(inputs["L0_ffn_w2"],
                                               np.float32).T),
        "a1h": np.ascontiguousarray(
            np.asarray(inputs["L0_ffn_a1"], np.float32)
            .reshape(E, 16, NDC, 128).transpose(3, 0, 2, 1).reshape(128, -1)),
        "a3h": np.ascontiguousarray(
            np.asarray(inputs["L0_ffn_a3"], np.float32)
            .reshape(E, 16, NDC, 128).transpose(3, 0, 2, 1).reshape(128, -1)),
        "a2h": np.ascontiguousarray(
            np.asarray(inputs["L0_ffn_a2"], np.float32)
            .reshape(E, 16, NFC, 128).transpose(3, 0, 2, 1).reshape(128, -1)),
        "b1h": np.ascontiguousarray(
            np.asarray(inputs["L0_ffn_b1"], np.float32)
            .transpose(2, 0, 1).reshape(16, -1)),
        "b3h": np.ascontiguousarray(
            np.asarray(inputs["L0_ffn_b3"], np.float32)
            .transpose(2, 0, 1).reshape(16, -1)),
        "b2h": np.ascontiguousarray(
            np.asarray(inputs["L0_ffn_b2"], np.float32)
            .transpose(2, 0, 1).reshape(16, -1)),
    }
    uu = np.arange(128)[:, None]
    ww = np.arange(MASKW)[None, :]
    maps = []
    for c in range(8):
        b, j = c // 4, c % 4
        qrows = np.array([hh * 512 + 32 * (4 * s + j) + r
                          for hh in range(H) for s in range(4)
                          for r in range(32)])
        orows = np.concatenate([np.arange(512 * (4 * s + j),
                                          512 * (4 * s + j) + 512)
                                for s in range(4)])
        m = dict(base)
        m["x_kv"] = x[b]
        m["xq"] = np.ascontiguousarray(x[b][qrows])
        m["x_own"] = np.ascontiguousarray(x[b][orows])
        m["strip"] = np.where(uu <= ww - 1920 + 512 * j, np.float32(0),
                              np.float32(NEG)).astype(np.float32)
        maps.append(m)
    return maps


def kernel(**inputs):
    maps = _host_inputs(inputs)
    nc = Build().build()
    res = run_bass_kernel_spmd(nc, maps, core_ids=list(range(8)))
    out = np.empty((2, T, D), np.float32)
    for c in range(8):
        b, j = c // 4, c % 4
        y = res.results[c]["y"]
        for s in range(4):
            out[b, 512 * (4 * s + j):512 * (4 * s + j) + 512] = \
                y[512 * s:512 * (s + 1)]
    return out
